# version 08 (final)
# baseline (speedup 1.0000x reference)
import numpy as np
import ml_dtypes
from contextlib import ExitStack

import concourse.bass as bass
import concourse.mybir as mybir
from concourse.bass_utils import run_bass_kernel_spmd

F32 = mybir.dt.float32
BF16 = mybir.dt.bfloat16
F16 = mybir.dt.float16
I32 = mybir.dt.int32
AF = mybir.ActivationFunctionType
ALU = mybir.AluOpType

H = 4096
DSSM = 8192
NH = 64
PD = 128
ND = 128
CONV = DSSM + 2 * ND
PROJ = DSSM + CONV + NH
EPS = 1e-5
T = 2048
L = 128
NCH = T // L
NKT = H // 128
NCT_X = DSSM // 128
TEXT = T + 3

DEBUG = ()
SIM_SKIP_CC = False


class Ctr:
    def __init__(self, sem):
        self.sem = sem
        self.n = 0

    def inc(self, instr, k=1):
        instr.then_inc(self.sem, k)
        self.n += k
        return self.n


def _wait(eng, *ctrs):
    for c in ctrs:
        if c.n > 0:
            eng.wait_ge(c.sem, c.n)


def _dma_barrier(eng, g):
    for c in g["dma_ctrs"]:
        if c.n > 0:
            eng.wait_ge(c.sem, c.n)


def build_program():
    nc = bass.Bass("TRN2", target_bir_lowering=False)

    xseg = nc.dram_tensor("xseg", [TEXT, H], F32, kind="ExternalInput").ap()
    w_in = nc.dram_tensor("w_in", [H, PROJ], F32, kind="ExternalInput").ap()
    cw = nc.dram_tensor("cw", [128, 66 * 4], F32, kind="ExternalInput").ap()
    dtb_i = nc.dram_tensor("dtb", [NH], F32, kind="ExternalInput").ap()
    alog_i = nc.dram_tensor("alog", [NH], F32, kind="ExternalInput").ap()
    dpar_i = nc.dram_tensor("dpar", [NH], F32, kind="ExternalInput").ap()
    nw_i = nc.dram_tensor("nw", [NCT_X, 128], F32, kind="ExternalInput").ap()
    w_out = nc.dram_tensor("w_out", [DSSM, H], F32, kind="ExternalInput").ap()
    fpk = nc.dram_tensor("fpk", [H, 2048], I32, kind="ExternalInput").ap()
    fsc = nc.dram_tensor("fsc", [H, 128], F16, kind="ExternalInput").ap()
    triu_i = nc.dram_tensor("triu", [128, 128], BF16, kind="ExternalInput").ap()
    ltri_i = nc.dram_tensor("ltri", [128, 128], BF16, kind="ExternalInput").ap()
    trif_i = nc.dram_tensor("trif", [128, 128], F32, kind="ExternalInput").ap()
    idb_i = nc.dram_tensor("idb", [128, 128], BF16, kind="ExternalInput").ap()
    idf_i = nc.dram_tensor("idf", [128, 128], F32, kind="ExternalInput").ap()
    eyef_i = nc.dram_tensor("eyef", [128, 128], F32, kind="ExternalInput").ap()
    segm_i = nc.dram_tensor("segm", [128, 4], F32, kind="ExternalInput").ap()

    out = nc.dram_tensor("out", [T, H], F32, kind="ExternalOutput").ap()

    u_d = nc.dram_tensor("u_d", [NCT_X, NCH, 128, 128], BF16, kind="Internal").ap()
    xfm_d = nc.dram_tensor("xfm_d", [NCT_X, 128, T], BF16, kind="Internal").ap()
    sz_d = nc.dram_tensor("sz_d", [NCT_X, 128, T], BF16, kind="Internal").ap()
    yg_d = nc.dram_tensor("yg_d", [NCT_X, NCH, 128, 128], BF16, kind="Internal").ap()
    mo_d = nc.dram_tensor("mo_d", [NCH, 128, H], F32, kind="Internal").ap()
    wdt_d = nc.dram_tensor("wdt_d", [NKT, 128, H], BF16, kind="Internal").ap()
    cc_in = nc.dram_tensor("cc_in", [128, DSSM + NH], F32, kind="Internal").ap()
    cc_out = nc.dram_tensor("cc_out", [512, DSSM + NH], F32, kind="Internal").ap()

    dbg = {}
    for name in DEBUG:
        shp = {
            "xlnt0": [128, TEXT], "dtfm": [NH, T], "S": [128, NH], "rdt": [128, NH],
            "drp": [128, NH], "bfm": [128, T], "cfm": [128, T], "xfm0": [128, T],
            "sz0": [128, T], "u00": [128, 128], "hloc": [128, DSSM], "hin": [128, DSSM],
            "ych0": [128, DSSM], "yg0": [128, DSSM], "rstd0": [128, 1],
            "mo0": [128, H], "x20": [128, H], "ffn0": [128, H], "wdt0": [128, H],
        }[name]
        dbg[name] = nc.dram_tensor("dbg_" + name, shp, F32, kind="ExternalOutput").ap()

    prog(nc, locals())
    return nc


def prog(nc, io):
    g = dict(io)
    st = ExitStack()
    E = st.enter_context

    dtb = E(nc.sbuf_tensor("dtb_s", [NH, 1], F32))
    expA = E(nc.sbuf_tensor("expA_s", [NH, 1], F32))
    negA = E(nc.sbuf_tensor("negA_s", [NH, 1], F32))
    drep64 = E(nc.sbuf_tensor("drep64", [128, NH], F32))
    nwT = E(nc.sbuf_tensor("nwT", [128, NCT_X], F32))
    triu = E(nc.sbuf_tensor("triu_s", [128, 128], BF16))
    ltri = E(nc.sbuf_tensor("ltri_s", [128, 128], BF16))
    trif = E(nc.sbuf_tensor("trif_s", [128, 128], F32))
    idb = E(nc.sbuf_tensor("idb_s", [128, 128], BF16))
    idf = E(nc.sbuf_tensor("idf_s", [128, 128], F32))
    eyef = E(nc.sbuf_tensor("eyef_s", [128, 128], F32))
    segm = E(nc.sbuf_tensor("segm_s", [128, 4], F32))
    onesb = E(nc.sbuf_tensor("onesb", [128, 1], BF16))
    nops = E(nc.sbuf_tensor("nops", [1, 4], F32))
    nwrows = E(nc.sbuf_tensor("nwrows", [NCT_X, 128], F32))
    cwt = E(nc.sbuf_tensor("cwt", [128, 66 * 4], F32))
    rstd_all = E(nc.sbuf_tensor("rstd_all", [128, NCH], F32))

    stn = Ctr(E(nc.semaphore("stn")))
    sv = Ctr(E(nc.semaphore("sv")))
    sa = Ctr(E(nc.semaphore("sa")))
    scl = Ctr(E(nc.semaphore("scl")))
    sone = Ctr(E(nc.semaphore("sone")))
    spl = Ctr(E(nc.semaphore("spl")))
    dma_ctrs = [sone]
    def new_sem(name):
        c = Ctr(E(nc.semaphore(name)))
        dma_ctrs.append(c)
        return c
    g_newsem = new_sem
    block = E(nc.Block())
    g.update(locals())
    g["new_sem"] = new_sem

    st25 = ExitStack()
    for _nm, _shp, _dt in (("dtt", [128, NCH * NH], F32), ("rdt", [128, NCH * NH], F32),
                           ("drp", [128, NCH * NH], F32), ("S_all", [128, NCH * NH], F32),
                           ("dseg", [128, NH], F32), ("rdts", [128, NCH * NH], F32),
                           ("bfm", [128, T], BF16),
                           ("cfm", [128, T], BF16), ("btt", [128, NCH * 128], BF16)):
        g[_nm] = st25.enter_context(nc.sbuf_tensor(_nm, _shp, _dt))
    phase_params(nc, g)
    st2 = ExitStack()
    g["xlnT"] = st2.enter_context(nc.sbuf_tensor("xlnT", [128, NKT * TEXT], BF16))
    phase_p1a(nc, g)
    phase_p1b(nc, g)
    phase_p1c(nc, g)
    st2.close()
    st3 = ExitStack()
    g["hT"] = st3.enter_context(nc.sbuf_tensor("hT", [128, DSSM], F32))
    phase_p1d(nc, g)
    phase_exch(nc, g)
    phase_p2a(nc, g)
    st3.close()
    st25.close()
    phase_p2b(nc, g)
    phase_dequant(nc, g)
    phase_p2c(nc, g)
    phase_final(nc, g)
    st.close()


def phase_params(nc, g):
    block, sv, sa, stn, sone = g["block"], g["sv"], g["sa"], g["stn"], g["sone"]

    @block.sync
    def _(s):
        sone.inc(s.dma_start(out=g["dtb"][:, 0], in_=g["dtb_i"][:]), 16)
        sone.inc(s.dma_start(out=g["expA"][:, 0], in_=g["alog_i"][:]), 16)
        sone.inc(s.dma_start(out=g["triu"][:], in_=g["triu_i"][:]), 16)
        sone.inc(s.dma_start(out=g["ltri"][:], in_=g["ltri_i"][:]), 16)
        sone.inc(s.dma_start(out=g["trif"][:], in_=g["trif_i"][:]), 16)
        sone.inc(s.dma_start(out=g["idb"][:], in_=g["idb_i"][:]), 16)
        sone.inc(s.dma_start(out=g["idf"][:], in_=g["idf_i"][:]), 16)
        sone.inc(s.dma_start(out=g["eyef"][:], in_=g["eyef_i"][:]), 16)
        sone.inc(s.dma_start(out=g["segm"][:, :], in_=g["segm_i"][:, :]), 16)
        sone.inc(s.dma_start(out=g["drep64"][0:1, :], in_=g["dpar_i"][:].unsqueeze(0)), 16)
        sone.inc(s.dma_start(out=g["nwrows"][:, :], in_=g["nw_i"][:, :]), 16)
        sone.inc(s.dma_start(out=g["cwt"][:, :], in_=g["cw"][:, :]), 16)
    g["SONE_TOTAL"] = sone.n

    @block.scalar
    def _(sc):
        sc.wait_ge(sone.sem, g["SONE_TOTAL"])
        sa.inc(sc.activation(out=g["expA"][:], in_=g["expA"][:], func=AF.Exp))

    @block.vector
    def _(v):
        _wait(v, sa)
        sv.inc(v.tensor_scalar(out=g["negA"][:], in0=g["expA"][:], scalar1=-1.0,
                               scalar2=None, op0=ALU.mult))
        sv.inc(v.memset(g["onesb"][:], 1.0))

    with nc.psum_tensor("pp", [128, NH], F32) as pp, \
         nc.psum_tensor("pnw", [128, NCT_X], F32) as pnw, \
         nc.sbuf_tensor("onesf", [1, 128], F32) as onesf:
        @block.vector
        def _(v):
            sv.inc(v.memset(onesf[:], 1.0))

        @block.tensor
        def _(t):
            _wait(t, sv)
            t.wait_ge(sone.sem, g["SONE_TOTAL"])
            stn.inc(t.matmul(pp[:, :], onesf[0:1, :], g["drep64"][0:1, :], start=True, stop=True))
            stn.inc(t.transpose(pnw[:, :], g["nwrows"][:, :], g["idf"][0:NCT_X, 0:NCT_X]))

        @block.vector
        def _(v):
            _wait(v, stn)
            sv.inc(v.tensor_copy(out=g["drep64"][:], in_=pp[:]))
            sv.inc(v.tensor_copy(out=g["nwT"][:], in_=pnw[:]))


def phase_p1a(nc, g):
    block = g["block"]
    sv, sa, stn = g["sv"], g["sa"], g["stn"]
    xlnT = g["xlnT"]
    sx = [g["new_sem"]("sx0"), g["new_sem"]("sx1")]

    with nc.sbuf_tensor("xch", [128, 2, H], F32) as xch, \
         nc.sbuf_tensor("xlnb", [128, H], BF16) as xlnb, \
         nc.sbuf_tensor("bnr", [128, 6 * 8], F32) as bnr, \
         nc.sbuf_tensor("bnagg", [128, 2], F32) as bnagg, \
         nc.sbuf_tensor("rs", [128, 1], F32) as rs, \
         nc.psum_tensor("ptr", [128, 2048], BF16) as ptr:

        free_gate = []

        def ln_rows(it, nrows, src_rows, dst_cols):
            slot = it % 2
            @block.sync
            def _(s):
                if it >= 2:
                    s.wait_ge(sv.sem, free_gate[it - 2])
                sx[slot].inc(s.dma_start(out=xch[0:nrows, slot, :],
                                         in_=g["xseg"][src_rows, :]), 16)

            @block.vector
            def _(v):
                v.wait_ge(sx[slot].sem, sx[slot].n)
                _wait(v, stn)
                for i in range(8):
                    sv.inc(v.bn_stats(out=bnr[0:nrows, i*6:(i+1)*6],
                                      in_=xch[0:nrows, slot, i*512:(i+1)*512]))
                _wait(v, sv)
                sv.inc(v.bn_aggr(out=bnagg[0:nrows, :], in_=bnr[0:nrows, :]))
                _wait(v, sv)
                sv.inc(v.tensor_scalar(out=rs[0:nrows, :], in0=bnagg[0:nrows, 1:2],
                                       scalar1=EPS, scalar2=None, op0=ALU.add))

            @block.scalar
            def _(sc):
                _wait(sc, sv)
                sa.inc(sc.activation(out=rs[0:nrows, :], in_=rs[0:nrows, :], func=AF.Ln))
                _wait(sc, sa)
                sa.inc(sc.activation(out=rs[0:nrows, :], in_=rs[0:nrows, :], func=AF.Exp,
                                     scale=-0.5))

            @block.vector
            def _(v):
                _wait(v, sa)
                sv.inc(v.tensor_scalar(out=xlnb[0:nrows, :], in0=xch[0:nrows, slot, :],
                                       scalar1=bnagg[0:nrows, 0:1], scalar2=rs[0:nrows, 0:1],
                                       op0=ALU.subtract, op1=ALU.mult))
            free_gate.append(sv.n)

            evac_done = {}
            for grp in range(NKT // 4):
                gp2 = grp % 2
                @block.tensor
                def _(t, grp=grp, gp2=gp2):
                    if grp == 0:
                        _wait(t, sv)
                    if grp >= 2:
                        t.wait_ge(sv.sem, evac_done[grp - 2])
                    for kk in range(4):
                        k = grp * 4 + kk
                        ins = t.transpose(ptr[:, gp2*1024 + kk*128: gp2*1024 + kk*128 + nrows],
                                          xlnb[0:nrows, k*128:(k+1)*128],
                                          g["idb"][0:nrows, 0:nrows])
                    stn.inc(ins)

                @block.vector
                def _(v, grp=grp, gp2=gp2):
                    _wait(v, stn)
                    for kk in range(4):
                        k = grp * 4 + kk
                        sv.inc(v.tensor_copy(
                            out=xlnT[:, k*TEXT + dst_cols : k*TEXT + dst_cols + nrows],
                            in_=ptr[:, gp2*1024 + kk*128: gp2*1024 + kk*128 + nrows]))
                    evac_done[grp] = sv.n

        ln_rows(0, 3, slice(0, 3), 0)
        for c in range(NCH):
            ln_rows(1 + c, 128, slice(3 + c * 128, 3 + (c + 1) * 128), 3 + c * 128)

    if "xlnt0" in g["dbg"]:
        dump(nc, g, g["dbg"]["xlnt0"], xlnT[:, 0:TEXT], F32)


_DUMP_N = [0]

def dump(nc, g, dst, src_ap, dt):
    block, sv = g["block"], g["sv"]
    P, Fn = src_ap.shape[0], int(np.prod(src_ap.shape[1:]))
    _DUMP_N[0] += 1
    sd = g["new_sem"]("sdmp%d" % _DUMP_N[0])
    CH = min(Fn, 1024)
    src2 = src_ap if len(src_ap.shape) == 2 else src_ap.rearrange("p a b -> p (a b)")
    with nc.sbuf_tensor("dmp%d" % _DUMP_N[0], [128, CH], F32) as dmp:
        for of in range(0, Fn, CH):
            n = min(CH, Fn - of)
            @block.vector
            def _(v, of=of, n=n):
                _wait(v, g["stn"], g["sa"], sv)
                _dma_barrier(v, g)
                v.wait_ge(sd.sem, sd.n)
                sv.inc(v.tensor_copy(out=dmp[0:P, 0:n], in_=src2[0:P, of:of+n]))

            @block.sync
            def _(s, of=of, n=n):
                _wait(s, sv)
                d2 = dst if len(dst.shape) == 2 else dst.unsqueeze(0)
                sd.inc(s.dma_start(out=d2[0:P, of:of+n], in_=dmp[0:P, 0:n]), 16)
            s_last = None

        @block.sync
        def _(s):
            s.wait_ge(sd.sem, sd.n)

        @block.vector
        def _(v):
            v.wait_ge(sd.sem, sd.n)
            sv.inc(v.memset(dmp[0:1, 0:1], 0.0))


def phase_p1b(nc, g):
    block = g["block"]
    sv, sa, stn = g["sv"], g["sa"], g["stn"]
    xlnT = g["xlnT"]
    s1b = g["new_sem"]("s1b")

    with nc.sbuf_tensor("wdtb", [128, NKT * NH], BF16) as wdtb, \
         nc.sbuf_tensor("dtfm", [NH, T], F32) as dtfm, \
         nc.sbuf_tensor("spt", [NH, T], F32) as spt, \
         nc.sbuf_tensor("ldafm", [NH, T], F32) as ldafm, \
         nc.sbuf_tensor("ldab", [128, NH], BF16) as ldab, \
         nc.sbuf_tensor("Rt", [128, NH], F32) as Rt, \
         nc.sbuf_tensor("rr", [128, NH], F32) as rr, \
         nc.psum_tensor("pdt", [NH, T], F32) as pdt, \
         nc.psum_tensor("ptt", [128, 256], F32) as ptt, \
         nc.psum_tensor("pcs", [128, 2 * NH], F32) as pcs:

        @block.gpsimd
        def _(gp):
            _wait(gp, g["sv"], g["stn"])
            for kq in range(4):
                src = g["w_in"][:, PROJ - NH:PROJ].rearrange(
                    "(kt p) c -> p kt c", p=128)[:, kq*8:(kq+1)*8, :]
                s1b.inc(gp.dma_start(out=wdtb.ap().rearrange(
                    "p (kt c) -> p kt c", c=NH)[:, kq*8:(kq+1)*8, :], in_=src), 16)
        S1B_TOTAL = s1b.n

        @block.tensor
        def _(t):
            t.wait_ge(s1b.sem, S1B_TOTAL)
            _wait(t, sv, stn)
            for k in range(NKT):
                for tb in range(4):
                    ins = t.matmul(pdt[:, tb*512:(tb+1)*512],
                                   wdtb[:, k*NH:(k+1)*NH],
                                   xlnT[:, k*TEXT + 3 + tb*512 : k*TEXT + 3 + (tb+1)*512],
                                   start=(k == 0), stop=(k == NKT - 1))
            stn.inc(ins)

        @block.vector
        def _(v):
            _wait(v, stn)
            v.wait_ge(g["sone"].sem, g["SONE_TOTAL"])
            sv.inc(v.tensor_scalar(out=dtfm[:], in0=pdt[:], scalar1=g["dtb"][0:NH, 0:1],
                                   scalar2=None, op0=ALU.add))

        if "dtfm" in g["dbg"]:
            dump(nc, g, g["dbg"]["dtfm"], dtfm[:, :], F32)

        @block.scalar
        def _(sc):
            _wait(sc, sv)
            sa.inc(sc.activation(out=spt[:], in_=dtfm[:], func=AF.Abs))
            _wait(sc, sa)
            sa.inc(sc.activation(out=spt[:], in_=spt[:], func=AF.Exp, scale=-1.0))
            _wait(sc, sa)
            sa.inc(sc.activation(out=spt[:], in_=spt[:], func=AF.Ln, bias=1.0))

        @block.vector
        def _(v):
            _wait(v, sa)
            sv.inc(v.tensor_scalar(out=dtfm[:], in0=dtfm[:], scalar1=0.0,
                                   scalar2=None, op0=ALU.max))
            _wait(v, sv)
            sv.inc(v.tensor_tensor(out=dtfm[:], in0=dtfm[:], in1=spt[:], op=ALU.add))
            _wait(v, sv)
            sv.inc(v.tensor_scalar(out=ldafm[:], in0=dtfm[:], scalar1=g["negA"][0:NH, 0:1],
                                   scalar2=None, op0=ALU.mult))

        for c in range(NCH):
            @block.tensor
            def _(t, c=c):
                _wait(t, sv)
                stn.inc(t.transpose(ptt[:, 0:64], dtfm[0:NH, c*128:(c+1)*128], g["idf"][0:NH, 0:NH]))
                stn.inc(t.transpose(ptt[:, 64:128], ldafm[0:NH, c*128:(c+1)*128], g["idf"][0:NH, 0:NH]))

            @block.vector
            def _(v, c=c):
                _wait(v, stn)
                sv.inc(v.tensor_copy(out=g["dtt"][:, c*NH:(c+1)*NH], in_=ptt[:, 0:64]))
                sv.inc(v.tensor_copy(out=ldab[:], in_=ptt[:, 64:128]))

            @block.tensor
            def _(t, c=c):
                _wait(t, sv)
                stn.inc(t.matmul(pcs[:, 0:NH], g["triu"][:], ldab[:], start=True, stop=True))
                stn.inc(t.matmul(pcs[:, NH:2*NH], g["ltri"][:], ldab[:], start=True, stop=True))

            @block.vector
            def _(v, c=c):
                _wait(v, stn)
                sv.inc(v.tensor_copy(out=g["S_all"][:, c*NH:(c+1)*NH], in_=pcs[:, 0:NH]))
                sv.inc(v.tensor_copy(out=Rt[:], in_=pcs[:, NH:2*NH]))
                _wait(v, sv)
                sv.inc(v.tensor_tensor(out=g["drp"][:, c*NH:(c+1)*NH],
                                       in0=g["S_all"][:, c*NH:(c+1)*NH], in1=Rt[:], op=ALU.add))

            @block.scalar
            def _(sc, c=c):
                _wait(sc, sv)
                sa.inc(sc.activation(out=g["drp"][:, c*NH:(c+1)*NH],
                                     in_=g["drp"][:, c*NH:(c+1)*NH], func=AF.Exp))
                sa.inc(sc.activation(out=rr[:], in_=Rt[:], func=AF.Exp))

            @block.vector
            def _(v, c=c):
                _wait(v, sa)
                sv.inc(v.tensor_tensor(out=g["rdt"][:, c*NH:(c+1)*NH], in0=rr[:],
                                       in1=g["dtt"][:, c*NH:(c+1)*NH], op=ALU.mult))

        @block.vector
        def _(v):
            _wait(v, sv, g["sa"])
            sv.inc(v.memset(g["rdts"][:, (NCH-1)*NH:NCH*NH], 1.0))
            for c in range(NCH - 2, -1, -1):
                _wait(v, sv)
                sv.inc(v.tensor_tensor(out=g["rdts"][:, c*NH:(c+1)*NH],
                                       in0=g["rdts"][:, (c+1)*NH:(c+2)*NH],
                                       in1=g["drp"][:, (c+1)*NH:(c+2)*NH], op=ALU.mult))
            _wait(v, sv)
            sv.inc(v.tensor_tensor(out=g["dseg"][:], in0=g["rdts"][:, 0:NH],
                                   in1=g["drp"][:, 0:NH], op=ALU.mult))
            _wait(v, sv)
            for c in range(NCH):
                sv.inc(v.tensor_tensor(out=g["rdts"][:, c*NH:(c+1)*NH],
                                       in0=g["rdts"][:, c*NH:(c+1)*NH],
                                       in1=g["rdt"][:, c*NH:(c+1)*NH], op=ALU.mult))

    for nm, src2 in (("S", g["S_all"]), ("rdt", g["rdt"]), ("drp", g["drp"])):
        if nm in g["dbg"]:
            dump(nc, g, g["dbg"][nm], src2[:, 0:NH], F32)


def phase_p1c(nc, g):
    block = g["block"]
    sv, sa, stn = g["sv"], g["sa"], g["stn"]
    xlnT = g["xlnT"]
    swb = [g["new_sem"]("swb0"), g["new_sem"]("swb1")]
    ssp = [g["new_sem"]("ssp0"), g["new_sem"]("ssp1")]

    order = [("B", 0, DSSM + DSSM), ("C", 0, DSSM + DSSM + ND)]
    order += [("x", i, DSSM + 128 * i) for i in range(NCT_X)]
    order += [("z", j, 128 * j) for j in range(64)]

    with nc.sbuf_tensor("wbuf", [128, 2, NKT * 128], BF16) as wbuf, \
         nc.sbuf_tensor("raw", [128, 2051], F32) as raw, \
         nc.sbuf_tensor("acc", [128, 2048], F32) as acc, \
         nc.sbuf_tensor("stage", [128, 2, 2048], BF16) as stage, \
         nc.psum_tensor("pA", [128, 2048], F32) as pA, \
         nc.psum_tensor("pH", [128, 512], F32) as pH, \
         nc.psum_tensor("ptb", [128, 128], BF16) as ptb:

        mm_done = {}
        conv_done = {}
        cd_sem = {}
        silu_done = {}

        for i, (kind, idx, c0) in enumerate(order):
            slot = i % 2
            is_xbc = kind in ("B", "C", "x")
            cd_sem[i] = "sa" if is_xbc else "sv"

            @block.gpsimd
            def _(gp, i=i, slot=slot, c0=c0):
                if i < 2:
                    _wait(gp, sv, stn)
                if i >= 2:
                    gp.wait_ge(stn.sem, mm_done[i - 2])
                for q in range(4):
                    srcw = g["w_in"][:, c0:c0 + 128].rearrange(
                        "(kt p) c -> p kt c", p=128)[:, q*8:(q+1)*8, :]
                    swb[slot].inc(gp.dma_start(
                        out=wbuf[:, slot, :].rearrange("p (kt c) -> p kt c", c=128)
                        [:, q*8:(q+1)*8, :], in_=srcw), 16)

            @block.tensor
            def _(t, i=i, slot=slot, is_xbc=is_xbc):
                t.wait_ge(swb[slot].sem, swb[slot].n)
                if i == 0:
                    _wait(t, sv, sa)
                if i >= 1:
                    if cd_sem[i - 1] == "sa":
                        t.wait_ge(sa.sem, conv_done[i - 1])
                    else:
                        t.wait_ge(sv.sem, conv_done[i - 1])
                for k in range(NKT):
                    lhs = wbuf[:, slot, k*128:(k+1)*128]
                    base = k * TEXT + (0 if is_xbc else 3)
                    for tb in range(4):
                        ins = t.matmul(pA[:, tb*512:(tb+1)*512], lhs,
                                       xlnT[:, base + tb*512: base + (tb+1)*512],
                                       start=(k == 0), stop=(k == NKT - 1))
                    if is_xbc:
                        ins = t.matmul(pH[:, 0:3], lhs,
                                       xlnT[:, k*TEXT + 2048: k*TEXT + 2051],
                                       start=(k == 0), stop=(k == NKT - 1))
                stn.inc(ins)
                mm_done[i] = stn.n

            if is_xbc:
                @block.scalar
                def _(sc, i=i):
                    _wait(sc, stn)
                    _wait(sc, sv)
                    sa.inc(sc.activation(out=raw[:, 0:2048], in_=pA[:, 0:2048], func=AF.Copy))
                    sa.inc(sc.activation(out=raw[:, 2048:2051], in_=pH[:, 0:3], func=AF.Copy))
                    conv_done[i] = None
                silu_psum_free = sa.n

                @block.vector
                def _(v, i=i, idx=idx, kind=kind, spf=silu_psum_free):
                    v.wait_ge(sa.sem, spf)
                    _wait(v, sa)
                    cwc = g["cwt"]
                    ctg = {"x": idx, "B": NCT_X, "C": NCT_X + 1}[kind]
                    sv.inc(v.tensor_scalar(out=acc[:, :], in0=raw[:, 0:2048],
                                           scalar1=cwc[:, 4*ctg:4*ctg+1],
                                           scalar2=None, op0=ALU.mult))
                    _wait(v, sv)
                    for j in (1, 2, 3):
                        sv.inc(v.scalar_tensor_tensor(
                            out=acc[:, 0:2048], in0=raw[:, j:2048 + j],
                            scalar=cwc[:, 4*ctg+j:4*ctg+j+1],
                            in1=acc[:, 0:2048], op0=ALU.mult, op1=ALU.add))
                        _wait(v, sv)
                conv_done[i] = silu_psum_free

                if kind == "x":
                    @block.scalar
                    def _(sc, i=i, slot=slot):
                        _wait(sc, sv)
                        if i >= 2:
                            sc.wait_ge(ssp[slot].sem, ssp[slot].n)
                        sa.inc(sc.activation(out=stage[:, slot, :], in_=acc[:, :],
                                             func=AF.Silu))
                        silu_done[i] = sa.n
                else:
                    dest = g["bfm"] if kind == "B" else g["cfm"]
                    @block.scalar
                    def _(sc, i=i, dest=dest):
                        _wait(sc, sv)
                        sa.inc(sc.activation(out=dest[:, :], in_=acc[:, :], func=AF.Silu))
                        silu_done[i] = sa.n
            else:
                @block.scalar
                def _(sc, i=i, slot=slot):
                    _wait(sc, stn)
                    if i >= 2:
                        sc.wait_ge(ssp[slot].sem, ssp[slot].n)
                    sa.inc(sc.activation(out=stage[:, slot, :], in_=pA[:, :], func=AF.Silu))
                    silu_done[i] = sa.n

                @block.vector
                def _(v, i=i):
                    _wait(v, sa)
                    sv.inc(v.memset(g["nops"][:, 0:1], 0.0))
                    conv_done[i] = sv.n

            if kind == "B":
                for c in range(NCH):
                    @block.tensor
                    def _(t, c=c, i=i):
                        t.wait_ge(sa.sem, silu_done[i])
                        _wait(t, sv)
                        stn.inc(t.transpose(ptb[:, :], g["bfm"][:, c*128:(c+1)*128],
                                            g["idb"][:, :]))

                    @block.vector
                    def _(v, c=c):
                        _wait(v, stn)
                        sv.inc(v.tensor_copy(out=g["btt"][:, c*128:(c+1)*128], in_=ptb[:, :]))

            if kind in ("x", "z"):
                @block.sync
                def _(s, i=i, slot=slot, idx=idx, kind=kind):
                    s.wait_ge(sa.sem, silu_done[i])
                    dst = g["xfm_d"][idx, :, :] if kind == "x" else g["sz_d"][idx, :, :]
                    ssp[slot].inc(s.dma_start(out=dst, in_=stage[:, slot, :]), 16)

        g["SSP_FIN"] = (ssp[0].n, ssp[1].n)
        g["ssp"] = ssp

    for nm, ctsel in (("bfm", None), ("cfm", None)):
        if nm in g["dbg"]:
            dump(nc, g, g["dbg"][nm], g[nm][:, :], F32)

def phase_p1d(nc, g):
    block = g["block"]
    sv, sa, stn = g["sv"], g["sa"], g["stn"]
    sxf = [g["new_sem"]("sxf0"), g["new_sem"]("sxf1")]
    sus = [g["new_sem"]("sus0"), g["new_sem"]("sus1")]
    g["sus"] = sus

    with nc.sbuf_tensor("xfb", [128, 2, T], BF16) as xfb, \
         nc.sbuf_tensor("ust", [128, 2, NCH * 128], BF16) as ust, \
         nc.sbuf_tensor("bdt16", [128, NCH * 128], BF16) as bdt16, \
         nc.psum_tensor("ptu", [128, 512], BF16) as ptu, \
         nc.psum_tensor("psS", [128, 1024], F32) as psS:

        @block.vector
        def _(v):
            _wait(v, sa, stn)
            sv.inc(v.memset(g["hT"][:, :], 0.0))

        tr_done = {}
        sp_gate = {}

        for ct in range(NCT_X):
            slot = ct % 2

            @block.sync
            def _(s, ct=ct, slot=slot):
                if ct < 2:
                    _wait(s, sv)
                    s.wait_ge(g["ssp"][0].sem, g["SSP_FIN"][0])
                    s.wait_ge(g["ssp"][1].sem, g["SSP_FIN"][1])
                else:
                    s.wait_ge(stn.sem, sp_gate[ct - 2])
                sxf[slot].inc(s.dma_start(out=xfb[:, slot, :],
                                          in_=g["xfm_d"][ct, :, :]), 16)

            for cg in range(4):
                @block.tensor
                def _(t, ct=ct, slot=slot, cg=cg):
                    t.wait_ge(sxf[slot].sem, sxf[slot].n)
                    _wait(t, sv, sa)
                    for cc in range(4):
                        c = cg * 4 + cc
                        ins = t.transpose(ptu[:, cc*128:(cc+1)*128],
                                          xfb[:, slot, c*128:(c+1)*128], g["idb"][:, :])
                    stn.inc(ins)

                @block.vector
                def _(v, ct=ct, slot=slot, cg=cg):
                    _wait(v, stn)
                    if cg == 0 and ct >= 2:
                        v.wait_ge(sus[slot].sem, sus[slot].n)
                    for cc in range(4):
                        c = cg * 4 + cc
                        sv.inc(v.tensor_copy(out=ust[:, slot, c*128:(c+1)*128],
                                             in_=ptu[:, cc*128:(cc+1)*128]))

            tr_done[ct] = sv.n

            @block.sync
            def _(s, ct=ct, slot=slot):
                s.wait_ge(sv.sem, tr_done[ct])
                sus[slot].inc(s.dma_start(
                    out=g["u_d"][ct, :, :, :].transpose([1, 0, 2]),
                    in_=ust[:, slot, :].rearrange("p (c q) -> p c q", q=128)), 16)

            hpar = ct % 2
            @block.vector
            def _(v, ct=ct):
                _wait(v, stn)
                for c in range(NCH):
                    sv.inc(v.tensor_scalar(
                        out=bdt16[:, c*128:(c+1)*128], in0=g["btt"][:, c*128:(c+1)*128],
                        scalar1=g["rdts"][:, c*NH + ct:c*NH + ct + 1],
                        scalar2=None, op0=ALU.mult))
            bd_done = sv.n

            @block.tensor
            def _(t, ct=ct, slot=slot, bd=bd_done, hpar=hpar):
                t.wait_ge(sv.sem, bd)
                _wait(t, sa)
                for c in range(NCH):
                    ins = t.matmul(psS[:, hpar*512:hpar*512+128],
                                   bdt16[:, c*128:(c+1)*128],
                                   ust[:, slot, c*128:(c+1)*128],
                                   start=(c == 0), stop=(c == NCH - 1))
                stn.inc(ins)
                sp_gate[ct] = stn.n

            @block.scalar
            def _(sc, ct=ct, hpar=hpar):
                _wait(sc, stn)
                sa.inc(sc.activation(out=g["hT"][:, ct*128:(ct+1)*128],
                                     in_=psS[:, hpar*512:hpar*512+128], func=AF.Copy))

    if "u00" in g["dbg"]:
        with nc.sbuf_tensor("u00b", [128, 128], BF16) as u00b:
            sdu = g["new_sem"]("sdu")
            @block.sync
            def _(s):
                _wait(s, sv)
                sdu.inc(s.dma_start(out=u00b[:, :], in_=g["u_d"][0, 0, :, :]), 16)
            @block.vector
            def _(v):
                v.wait_ge(sdu.sem, sdu.n)
                sv.inc(v.memset(g["nops"][:, 0:1], 0.0))
            dump(nc, g, g["dbg"]["u00"], u00b[:, :], F32)
    if "sz0" in g["dbg"]:
        with nc.sbuf_tensor("sz0b", [128, T], BF16) as sz0b:
            sdz = g["new_sem"]("sdz")
            @block.sync
            def _(s):
                _wait(s, sv)
                s.wait_ge(g["ssp"][0].sem, g["SSP_FIN"][0])
                s.wait_ge(g["ssp"][1].sem, g["SSP_FIN"][1])
                sdz.inc(s.dma_start(out=sz0b[:, :], in_=g["sz_d"][0, :, :]), 16)
            @block.vector
            def _(v):
                v.wait_ge(sdz.sem, sdz.n)
                sv.inc(v.memset(g["nops"][:, 0:1], 0.0))
            dump(nc, g, g["dbg"]["sz0"], sz0b[:, :], F32)
    if "hloc" in g["dbg"]:
        dump(nc, g, g["dbg"]["hloc"], g["hT"][:, :], F32)

def phase_exch(nc, g):
    block = g["block"]
    sv, sa, stn, scl = g["sv"], g["sa"], g["stn"], g["scl"]
    sxc = g["new_sem"]("sxc")
    sgh = g["new_sem"]("sgh")

    with nc.sbuf_tensor("gh", [128, DSSM + NH], F32) as gh, \
         nc.sbuf_tensor("dsg4", [128, 4 * NH], F32) as dsg4, \
         nc.sbuf_tensor("fl", [128, 3 * NH], F32) as fl, \
         nc.sbuf_tensor("cf", [128, 4 * NH], F32) as cf, \
         nc.sbuf_tensor("oms", [128, 4], F32) as oms:

        @block.sync
        def _(s):
            _wait(s, sv, g["sa"], g["stn"])
            _dma_barrier(s, g)
            sxc.inc(s.dma_start(out=g["cc_in"][:, 0:DSSM], in_=g["hT"][:, :]), 16)
            sxc.inc(s.dma_start(out=g["cc_in"][:, DSSM:], in_=g["dseg"][:, :]), 16)
        CC_STAGE = sxc.n

        @block.gpsimd
        def _(gp):
            gp.wait_ge(sxc.sem, CC_STAGE)
            if SIM_SKIP_CC:
                for l in range(4):
                    scl.inc(gp.dma_start(out=g["cc_out"][l*128:(l+1)*128, :],
                                         in_=g["cc_in"][:, :]), 16)
            else:
                scl.inc(gp.collective_compute(
                    "AllGather", ALU.bypass,
                    ins=[g["cc_in"][:, :].opt()], outs=[g["cc_out"][:, :].opt()],
                    replica_groups=[[0, 1, 2, 3], [4, 5, 6, 7]],
                ), 1)

        @block.sync
        def _(s):
            s.wait_ge(scl.sem, scl.n if not SIM_SKIP_CC else scl.n)
            for l in range(4):
                sxc.inc(s.dma_start(out=dsg4[:, l*NH:(l+1)*NH],
                                    in_=g["cc_out"][l*128:(l+1)*128, DSSM:]), 16)
        CC_D = sxc.n

        @block.vector
        def _(v):
            v.wait_ge(sxc.sem, CC_D)
            sv.inc(v.tensor_scalar(out=oms[:, :], in0=g["segm"][:, :], scalar1=-1.0,
                                   scalar2=1.0, op0=ALU.mult, op1=ALU.add))
            _wait(v, sv)
            for l in (1, 2, 3):
                sv.inc(v.tensor_scalar(out=fl[:, (l-1)*NH:l*NH], in0=dsg4[:, l*NH:(l+1)*NH],
                                       scalar1=g["segm"][:, l:l+1], scalar2=oms[:, l:l+1],
                                       op0=ALU.mult, op1=ALU.add))
            _wait(v, sv)
            sv.inc(v.memset(cf[:, 3*NH:4*NH], 1.0))
            _wait(v, sv)
            sv.inc(v.tensor_copy(out=cf[:, 2*NH:3*NH], in_=fl[:, 2*NH:3*NH]))
            _wait(v, sv)
            sv.inc(v.tensor_tensor(out=cf[:, 1*NH:2*NH], in0=fl[:, 1*NH:2*NH],
                                   in1=cf[:, 2*NH:3*NH], op=ALU.mult))
            _wait(v, sv)
            sv.inc(v.tensor_tensor(out=cf[:, 0:NH], in0=fl[:, 0:NH],
                                   in1=cf[:, 1*NH:2*NH], op=ALU.mult))
            _wait(v, sv)
            for j in range(4):
                sv.inc(v.tensor_scalar(out=cf[:, j*NH:(j+1)*NH], in0=cf[:, j*NH:(j+1)*NH],
                                       scalar1=g["segm"][:, j:j+1], scalar2=None,
                                       op0=ALU.mult))
            _wait(v, sv)
            sv.inc(v.memset(g["hT"][:, :], 0.0))

        for j in range(4):
            @block.sync
            def _(s, j=j):
                _wait(s, sv)
                sgh.inc(s.dma_start(out=gh[:, 0:DSSM],
                                    in_=g["cc_out"][j*128:(j+1)*128, 0:DSSM]), 16)

            @block.vector
            def _(v, j=j):
                v.wait_ge(sgh.sem, sgh.n)
                for h in range(NH):
                    _wait(v, sv)
                    sv.inc(v.scalar_tensor_tensor(
                        out=g["hT"][:, h*128:(h+1)*128], in0=gh[:, h*128:(h+1)*128],
                        scalar=cf[:, j*NH + h:j*NH + h + 1],
                        in1=g["hT"][:, h*128:(h+1)*128], op0=ALU.mult, op1=ALU.add))

    if "hin" in g["dbg"]:
        dump(nc, g, g["dbg"]["hin"], g["hT"][:, :], F32)


def phase_p2a(nc, g):
    block = g["block"]
    sv, sa, stn = g["sv"], g["sa"], g["stn"]
    su2 = [g["new_sem"]("su2_0"), g["new_sem"]("su2_1")]
    ssz2 = g["new_sem"]("ssz2")
    sfl = g["new_sem"]("sfl")
    sygw = g["new_sem"]("sygw")
    g["sygw"] = sygw

    from contextlib import ExitStack as _ES
    with _ES() as _stk:
        _E = _stk.enter_context
        ub = _E(nc.sbuf_tensor("ub", [128, DSSM], BF16))
        szb = _E(nc.sbuf_tensor("szb", [128, DSSM], BF16))
        mf = _E(nc.sbuf_tensor("mf", [128, DSSM], BF16))
        ccd = _E(nc.sbuf_tensor("ccd", [128, DSSM], BF16))
        hTb = _E(nc.sbuf_tensor("hTb", [128, DSSM], BF16))
        pairA = _E(nc.sbuf_tensor("pairA", [2, 4096], F32))
        pairB = _E(nc.sbuf_tensor("pairB", [2, 4096], F32))
        gmt = _E(nc.sbuf_tensor("gmt", [128, 128], BF16))
        stb = _E(nc.sbuf_tensor("stb", [NH, 128], F32))
        steb = _E(nc.sbuf_tensor("steb", [NH, 128], BF16))
        cdf = _E(nc.sbuf_tensor("cdf", [1, 2048], BF16))
        onesrow = _E(nc.sbuf_tensor("onesrow", [1, 128], BF16))
        dbig = _E(nc.sbuf_tensor("dbig", [128, 4096], BF16))
        dbig2 = _E(nc.sbuf_tensor("dbig2", [128, 4096], BF16))
        bdt8 = _E(nc.sbuf_tensor("bdt8", [128, 1024], BF16))
        ssrow = _E(nc.sbuf_tensor("ssrow", [1, 128], F32))
        rstd_r = _E(nc.sbuf_tensor("rstd_r", [1, 128], F32))
        pAll = _E(nc.psum_tensor("pAll", [128, 4096], F32))
        pG = pAll[:, 0:128]
        pST = pAll[0:NH, 128:256]
        pRow = pAll[0:1, 3584:3712]
        pCol = pAll[:, 3712:3720]

        sflP = g["new_sem"]("sflP")
        @block.vector
        def _(v):
            _wait(v, sv)
            sv.inc(v.memset(pairA[0:1, :], 1.0))
            sv.inc(v.memset(onesrow[:, :], 1.0))
            sv.inc(v.memset(stb[:, :], -1.0))
        @block.gpsimd
        def _(gp):
            _wait(gp, sv, stn, sa)
            sflP.inc(gp.dma_start(out=pairB[1:2, :], in_=stb[0:32, :]), 16)
        INIT_FL = sflP.n

        ych_free = {}
        ub_inflight = {}

        for c in range(NCH):
            par = c % 2
            @block.sync
            def _(s, c=c, par=par):
                if c == 0:
                    _wait(s, sv, stn, sa)
                else:
                    s.wait_ge(stn.sem, ub_inflight[c - 1])
                su2[par].inc(s.dma_start(out=ub[:, :],
                                         in_=g["u_d"][:, c, :, :].transpose([1, 0, 2])), 16)
                for q in range(4):
                    ssz2.inc(s.dma_start(
                        out=szb.ap().rearrange("p (ct t) -> p ct t", t=128)[:, q*16:(q+1)*16, :],
                        in_=g["sz_d"][q*16:(q+1)*16, :, c*128:(c+1)*128].transpose([1, 0, 2])), 16)

            @block.tensor
            def _(t, c=c):
                _wait(t, sv, sa)
                if c == 0:
                    t.wait_ge(sflP.sem, INIT_FL)
                stn.inc(t.matmul(pG, g["bfm"][:, c*128:(c+1)*128],
                                 g["cfm"][:, c*128:(c+1)*128], start=True, stop=True))
                stn.inc(t.transpose(pST, g["S_all"][:, c*NH:(c+1)*NH],
                                    g["idf"][0:128, 0:128]))

            @block.vector
            def _(v, c=c):
                _wait(v, stn)
                sv.inc(v.tensor_tensor(out=gmt[:, :], in0=pG, in1=g["trif"][:, :],
                                       op=ALU.mult))
                sv.inc(v.tensor_copy(out=stb[:, :], in_=pST))

            @block.scalar
            def _(sc, c=c):
                _wait(sc, sv)
                sa.inc(sc.activation(out=steb[:, :], in_=stb[:, :], func=AF.Exp))

            dmm_done = {}
            for half in range(2):
                @block.gpsimd
                def _(gp, c=c, half=half):
                    _wait(gp, sv)
                    if half == 1:
                        gp.wait_ge(stn.sem, dmm_done[0])
                    sflP.inc(gp.dma_start(out=pairA[1:2, :],
                                          in_=stb[half*32:half*32+32, :]), 16)
                    sflP.inc(gp.dma_start(out=pairB[0:1, :],
                                          in_=stb[half*32:half*32+32, :]), 16)

                @block.tensor
                def _(t, c=c, half=half):
                    t.wait_ge(sflP.sem, sflP.n)
                    if half == 0:
                        _wait(t, sv)
                    else:
                        _wait(t, sv)
                    for hl in range(32):
                        ins = t.matmul(pAll[:, hl*128:(hl+1)*128],
                                       pairA[0:2, hl*128:(hl+1)*128],
                                       pairB[0:2, hl*128:(hl+1)*128],
                                       start=True, stop=True)
                    stn.inc(ins)
                    dmm_done[half] = stn.n

                @block.vector
                def _(v, c=c, half=half):
                    _wait(v, stn, sa)
                    sv.inc(v.tensor_scalar(out=dbig[:, :], in0=pAll[:, 0:4096],
                                           scalar1=0.0, scalar2=None, op0=ALU.min))

                @block.scalar
                def _(sc, c=c, half=half):
                    _wait(sc, sv)
                    if half == 0 and c > 0:
                        sc.wait_ge(sygw.sem, sygw.n)
                    sa.inc(sc.activation(out=dbig[:, :], in_=dbig[:, :], func=AF.Exp))
                    _wait(sc, sa)
                    for hl in range(32):
                        h = half * 32 + hl
                        sa.inc(sc.activation(out=mf[:, (half*32+hl)*128:(half*32+hl+1)*128],
                                             in_=dbig[:, hl*128:(hl+1)*128],
                                             func=AF.Copy,
                                             scale=g["dtt"][:, c*NH + h:c*NH + h + 1]))

                @block.gpsimd
                def _(gp, c=c, half=half):
                    _wait(gp, sa)
                    for hl in range(32):
                        h = half * 32 + hl
                        g["spl"].inc(gp.tensor_tensor(out=mf[:, h*128:(h+1)*128],
                                                      in0=mf[:, h*128:(h+1)*128],
                                                      in1=gmt[:, :], op=ALU.mult))

                @block.vector
                def _(v, c=c, half=half):
                    _wait(v, g["spl"])
                    for hl in range(32):
                        h = half * 32 + hl
                        sv.inc(v.scalar_tensor_tensor(
                            out=mf[:, h*128:(h+1)*128], in0=g["eyef"][:, :],
                            scalar=g["drep64"][:, h:h+1],
                            in1=mf[:, h*128:(h+1)*128], op0=ALU.mult, op1=ALU.add))

            for rnd in range(4):
                h0 = rnd * 16
                rpar = rnd % 2
                @block.gpsimd
                def _(gp, c=c, h0=h0):
                    _wait(gp, sa, stn)
                    sfl.inc(gp.dma_start(out=cdf[0:1, :], in_=steb[h0:h0+16, :]), 16)

                @block.tensor
                def _(t, rpar=rpar, c=c, rnd=rnd):
                    t.wait_ge(sfl.sem, sfl.n)
                    _wait(t, sv)
                    for qq in range(4):
                        stn.inc(t.matmul(pAll[:, rpar*2048 + qq*512:rpar*2048 + (qq+1)*512],
                                         onesrow[0:1, :], cdf[0:1, qq*512:(qq+1)*512],
                                         start=True, stop=True))

                @block.vector
                def _(v, c=c, h0=h0, rpar=rpar):
                    _wait(v, stn)
                    for hh in range(16):
                        h = h0 + hh
                        sv.inc(v.tensor_tensor(
                            out=ccd[:, h*128:(h+1)*128],
                            in0=g["cfm"][:, c*128:(c+1)*128],
                            in1=pAll[:, rpar*2048 + hh*128:rpar*2048 + (hh+1)*128],
                            op=ALU.mult))

            @block.vector
            def _(v, c=c):
                _wait(v, sv)
                sv.inc(v.tensor_copy(out=hTb[:, :], in_=g["hT"][:, :]))

            yev = {}
            for grp in range(16):
                h0 = grp * 4
                gpar = grp % 2
                @block.tensor
                def _(t, c=c, h0=h0, gpar=gpar, par=par, grp=grp):
                    if grp == 0:
                        t.wait_ge(su2[par].sem, su2[par].n)
                        _wait(t, sv, sa)
                    if grp >= 2:
                        t.wait_ge(sa.sem, yev[grp - 2])
                    for hh in range(4):
                        h = h0 + hh
                        t.matmul(pAll[:, 2048 + gpar*512 + hh*128: 2048 + gpar*512 + (hh+1)*128],
                                 ub[:, h*128:(h+1)*128], mf[:, h*128:(h+1)*128],
                                 start=True, stop=False)
                        stn.inc(t.matmul(pAll[:, 2048 + gpar*512 + hh*128: 2048 + gpar*512 + (hh+1)*128],
                                         hTb[:, h*128:(h+1)*128], ccd[:, h*128:(h+1)*128],
                                         start=False, stop=True))

                @block.scalar
                def _(sc, h0=h0, gpar=gpar):
                    _wait(sc, stn)
                    sa.inc(sc.activation(out=ccd[:, h0*128:(h0+4)*128],
                                         in_=pAll[:, 2048 + gpar*512:2048 + (gpar+1)*512],
                                         func=AF.Copy))
                    yev[grp] = sa.n

            if c == 0 and "ych0" in g["dbg"]:
                dump(nc, g, g["dbg"]["ych0"], ccd[:, :], F32)

            @block.vector
            def _(v, c=c):
                _wait(v, sv, sa)
                v.wait_ge(ssz2.sem, ssz2.n)
                sv.inc(v.tensor_tensor(out=ccd[:, :], in0=ccd[:, :], in1=szb[:, :],
                                       op=ALU.mult))

            @block.scalar
            def _(sc, c=c):
                _wait(sc, sv)
                sa.inc(sc.activation(out=dbig[:, 0:4096], in_=ccd[:, 0:4096], func=AF.Square))
                sa.inc(sc.activation(out=dbig2[:, 0:4096], in_=ccd[:, 4096:8192], func=AF.Square))

            @block.tensor
            def _(t, c=c):
                _wait(t, sa, sv)
                for ct in range(NCT_X):
                    srcq = dbig if ct < 32 else dbig2
                    ins = t.matmul(pRow, g["onesb"][:, 0:1],
                                   srcq[:, (ct % 32)*128:(ct % 32 + 1)*128],
                                   start=(ct == 0), stop=(ct == NCT_X - 1))
                stn.inc(ins)

            @block.vector
            def _(v, c=c):
                _wait(v, stn)
                sv.inc(v.tensor_scalar(out=ssrow[:, :], in0=pRow,
                                       scalar1=1.0 / DSSM, scalar2=EPS,
                                       op0=ALU.mult, op1=ALU.add))
            @block.scalar
            def _(sc, c=c):
                _wait(sc, sv)
                sa.inc(sc.activation(out=rstd_r[:, :], in_=ssrow[:, :], func=AF.Ln))
                _wait(sc, sa)
                sa.inc(sc.activation(out=rstd_r[:, :], in_=rstd_r[:, :], func=AF.Exp,
                                     scale=-0.5))
            @block.tensor
            def _(t, c=c):
                _wait(t, sa, sv)
                stn.inc(t.transpose(pCol[:, 0:1], rstd_r[0:1, :], g["idf"][0:1, 0:1]))
            @block.vector
            def _(v, c=c):
                _wait(v, stn)
                sv.inc(v.tensor_copy(out=g["rstd_all"][:, c:c+1], in_=pCol[:, 0:1]))
                for ct in range(NCT_X):
                    sv.inc(v.tensor_scalar(out=mf[:, ct*128:(ct+1)*128],
                                           in0=ccd[:, ct*128:(ct+1)*128],
                                           scalar1=g["nwT"][:, ct:ct+1],
                                           scalar2=None, op0=ALU.mult))
                ych_free[c] = sv.n

            if c == 0 and "yg0" in g["dbg"]:
                dump(nc, g, g["dbg"]["yg0"], mf[:, :], F32)
            if c == 0 and "rstd0" in g["dbg"]:
                dump(nc, g, g["dbg"]["rstd0"], g["rstd_all"][:, 0:1], F32)

            @block.sync
            def _(s, c=c):
                s.wait_ge(sv.sem, ych_free[c])
                for q in range(4):
                    sygw.inc(s.dma_start(
                        out=g["yg_d"][q*16:(q+1)*16, c, :, :].transpose([1, 0, 2]),
                        in_=mf.ap().rearrange("p (ct t) -> p ct t", t=128)
                        [:, q*16:(q+1)*16, :]), 16)

            rec2 = {}
            for rnd in range(8):
                @block.vector
                def _(v, c=c, rnd=rnd):
                    _wait(v, stn)
                    for hh in range(8):
                        h = rnd * 8 + hh
                        sv.inc(v.tensor_scalar(
                            out=bdt8[:, hh*128:(hh+1)*128],
                            in0=g["btt"][:, c*128:(c+1)*128],
                            scalar1=g["rdt"][:, c*NH + h:c*NH + h + 1],
                            scalar2=None, op0=ALU.mult))
                bd8 = sv.n

                for gq in range(2):
                    gpar2 = (rnd * 2 + gq) % 2
                    @block.tensor
                    def _(t, c=c, rnd=rnd, gq=gq, gpar2=gpar2, bd8=bd8):
                        if gq == 0:
                            t.wait_ge(sv.sem, bd8)
                        key = rnd * 2 + gq
                        if key >= 2:
                            t.wait_ge(sv.sem, rec2[key - 2])
                        for hh in range(4):
                            hq = gq * 4 + hh
                            h = rnd * 8 + hq
                            ins = t.matmul(pAll[:, gpar2*512 + hh*128:gpar2*512 + (hh+1)*128],
                                           bdt8[:, hq*128:(hq+1)*128],
                                           ub[:, h*128:(h+1)*128], start=True, stop=True)
                        stn.inc(ins)
                        if rnd == 7 and gq == 1:
                            ub_inflight[c] = stn.n

                    @block.vector
                    def _(v, c=c, rnd=rnd, gq=gq, gpar2=gpar2):
                        _wait(v, stn)
                        for hh in range(4):
                            h = rnd * 8 + gq * 4 + hh
                            sv.inc(v.scalar_tensor_tensor(
                                out=g["hT"][:, h*128:(h+1)*128],
                                in0=g["hT"][:, h*128:(h+1)*128],
                                scalar=g["drp"][:, c*NH + h:c*NH + h + 1],
                                in1=pAll[:, gpar2*512 + hh*128:gpar2*512 + (hh+1)*128],
                                op0=ALU.mult, op1=ALU.add))
                        rec2[rnd * 2 + gq] = sv.n

    g["YGW_FIN"] = sygw.n


def phase_p2b(nc, g):
    block = g["block"]
    sv, sa, stn = g["sv"], g["sa"], g["stn"]
    sygr = g["new_sem"]("sygr")
    swo = [g["new_sem"]("swo0"), g["new_sem"]("swo1")]
    smo = [g["new_sem"]("smo0"), g["new_sem"]("smo1")]
    NKT2 = DSSM // 128

    with nc.sbuf_tensor("ygb", [128, NKT2 * 1024], BF16) as ygb, \
         nc.sbuf_tensor("wob", [128, 2, 4 * 512], BF16) as wob, \
         nc.sbuf_tensor("mstage", [128, 2, 512], F32) as mstage, \
         nc.psum_tensor("pW", [128, 8 * 512], F32) as pW:

        mm_k = {}
        for oct_ in range(2):
            @block.sync
            def _(s, oct_=oct_):
                if oct_ == 0:
                    _wait(s, sv, sa, stn)
                    _dma_barrier(s, g)
                else:
                    _wait(s, sv)
                    _wait(s, stn)
                for ct in range(NKT2):
                    sygr.inc(s.dma_start(
                        out=ygb[:, ct*1024:(ct+1)*1024].rearrange("p (c t) -> p c t", t=128),
                        in_=g["yg_d"][ct, oct_*8:(oct_+1)*8, :, :].transpose([1, 0, 2])), 16)

            KB = 4
            for jb in range(8):
                for kq in range(NKT2 // KB):
                    slot = kq % 2
                    @block.gpsimd
                    def _(gp, oct_=oct_, jb=jb, kq=kq, slot=slot):
                        if oct_ == 0 and jb == 0 and kq < 2:
                            _wait(gp, sv, stn)
                        key = (jb + oct_ * 8) * (NKT2 // KB) + kq
                        if key >= 2:
                            gp.wait_ge(stn.sem, mm_k[key - 2])
                        srcw = g["w_out"][kq*KB*128:(kq+1)*KB*128, jb*512:(jb+1)*512]
                        swo[slot].inc(gp.dma_start(
                            out=wob[:, slot, :].rearrange("p (kk c) -> p kk c", kk=KB),
                            in_=srcw.rearrange("(kk p) c -> p kk c", p=128)), 16)

                    @block.tensor
                    def _(t, oct_=oct_, jb=jb, kq=kq, slot=slot):
                        t.wait_ge(swo[slot].sem, swo[slot].n)
                        t.wait_ge(sygr.sem, sygr.n)
                        if kq == 0:
                            _wait(t, sa)
                        for kk in range(KB):
                            k = kq * KB + kk
                            for cc in range(8):
                                ins = t.matmul(pW[:, cc*512:(cc+1)*512],
                                               ygb[:, k*1024 + cc*128:k*1024 + (cc+1)*128],
                                               wob[:, slot, kk*512:(kk+1)*512],
                                               start=(k == 0), stop=(k == NKT2 - 1))
                        stn.inc(ins)
                        mm_k[(jb + oct_ * 8) * (NKT2 // KB) + kq] = stn.n

                for cc in range(8):
                    eslot = cc % 2
                    @block.scalar
                    def _(sc, oct_=oct_, jb=jb, cc=cc, eslot=eslot):
                        _wait(sc, stn)
                        if not (oct_ == 0 and jb == 0 and cc < 2):
                            sc.wait_ge(smo[eslot].sem, smo[eslot].n)
                        c = oct_ * 8 + cc
                        sa.inc(sc.activation(out=mstage[:, eslot, :], in_=pW[:, cc*512:(cc+1)*512],
                                             func=AF.Copy, scale=g["rstd_all"][:, c:c+1]))

                    @block.sync
                    def _(s, oct_=oct_, jb=jb, cc=cc, eslot=eslot):
                        _wait(s, sa)
                        c = oct_ * 8 + cc
                        smo[eslot].inc(s.dma_start(out=g["mo_d"][c, :, jb*512:(jb+1)*512],
                                                   in_=mstage[:, eslot, :]), 16)

    g["SMO_FIN"] = (smo[0], smo[1])
    if "mo0" in g["dbg"]:
        with nc.sbuf_tensor("mo0b", [128, H], F32) as mo0b:
            sdm = g["new_sem"]("sdm")
            @block.sync
            def _(s):
                s.wait_ge(smo[0].sem, smo[0].n)
                s.wait_ge(smo[1].sem, smo[1].n)
                sdm.inc(s.dma_start(out=mo0b[:, :], in_=g["mo_d"][0, :, :]), 16)
            @block.vector
            def _(v):
                v.wait_ge(sdm.sem, sdm.n)
                sv.inc(v.memset(g["nops"][:, 0:1], 0.0))
            dump(nc, g, g["dbg"]["mo0"], mo0b[:, :], F32)

def phase_dequant(nc, g):
    block = g["block"]
    sv, sa, stn = g["sv"], g["sa"], g["stn"]
    spk = [g["new_sem"]("spk0"), g["new_sem"]("spk1")]
    swd = [g["new_sem"]("swd0"), g["new_sem"]("swd1")]
    g["swd"] = swd

    with nc.sbuf_tensor("pkb", [128, 2, 2048], I32) as pkb, \
         nc.sbuf_tensor("scb", [128, 2, 128], F16) as scb, \
         nc.sbuf_tensor("srep", [128, 2048], F32) as srep, \
         nc.sbuf_tensor("loint", [128, 2048], I32) as loint, \
         nc.sbuf_tensor("hiint", [128, 2048], I32) as hiint, \
         nc.sbuf_tensor("wqt", [128, H], BF16) as wqt, \
         nc.sbuf_tensor("wdst", [128, 2, NKT * 128], BF16) as wdst, \
         nc.psum_tensor("pT2", [128, 512], BF16) as pT2:

        tr_done = {}
        for nt in range(NKT):
            slot = nt % 2
            @block.sync
            def _(s, nt=nt, slot=slot):
                if nt < 2:
                    _wait(s, sv, sa, stn)
                    _dma_barrier(s, g)
                else:
                    s.wait_ge(sv.sem, tr_done[nt - 2])
                spk[slot].inc(s.dma_start(out=pkb[:, slot, :],
                                          in_=g["fpk"][nt*128:(nt+1)*128, :]), 16)
                spk[slot].inc(s.dma_start(out=scb[:, slot, :],
                                          in_=g["fsc"][nt*128:(nt+1)*128, :]), 16)

            @block.vector
            def _(v, nt=nt, slot=slot):
                v.wait_ge(spk[slot].sem, spk[slot].n)
                _wait(v, stn)
                for i in range(16):
                    sv.inc(v.tensor_copy(out=srep.ap().rearrange("p (b i) -> p b i", i=16)[:, :, i],
                                         in_=scb[:, slot, :]))
                sv.inc(v.tensor_scalar(out=loint[:, :], in0=pkb[:, slot, :], scalar1=15,
                                       scalar2=None, op0=ALU.bitwise_and))
                sv.inc(v.tensor_scalar(out=hiint[:, :], in0=pkb[:, slot, :], scalar1=4,
                                       scalar2=15, op0=ALU.arith_shift_right,
                                       op1=ALU.bitwise_and))
                _wait(v, sv)
                sv.inc(v.scalar_tensor_tensor(out=wqt[:, 0:H:2], in0=loint[:, :], scalar=-8.0,
                                              in1=srep[:, :], op0=ALU.add, op1=ALU.mult))
                sv.inc(v.scalar_tensor_tensor(out=wqt[:, 1:H:2], in0=hiint[:, :], scalar=-8.0,
                                              in1=srep[:, :], op0=ALU.add, op1=ALU.mult))

            for kg in range(8):
                @block.tensor
                def _(t, nt=nt, kg=kg, slot=slot):
                    _wait(t, sv)
                    for kk in range(4):
                        ks = kg * 4 + kk
                        ins = t.transpose(pT2[:, kk*128:(kk+1)*128],
                                          wqt[:, ks*128:(ks+1)*128], g["idb"][:, :])
                    stn.inc(ins)

                @block.vector
                def _(v, nt=nt, kg=kg, slot=slot):
                    _wait(v, stn)
                    if kg == 0 and nt >= 2:
                        v.wait_ge(swd[slot].sem, swd[slot].n)
                    for kk in range(4):
                        ks = kg * 4 + kk
                        sv.inc(v.tensor_copy(out=wdst[:, slot, ks*128:(ks+1)*128],
                                             in_=pT2[:, kk*128:(kk+1)*128]))
            tr_done[nt] = sv.n

            @block.sync
            def _(s, nt=nt, slot=slot):
                s.wait_ge(sv.sem, tr_done[nt])
                swd[slot].inc(s.dma_start(
                    out=g["wdt_d"][:, :, nt*128:(nt+1)*128].transpose([1, 0, 2]),
                    in_=wdst[:, slot, :].rearrange("p (ks n) -> p ks n", n=128)), 16)

    if "wdt0" in g["dbg"]:
        with nc.sbuf_tensor("wd0b", [128, H], BF16) as wd0b:
            sdw = g["new_sem"]("sdw")
            @block.sync
            def _(s):
                s.wait_ge(swd[0].sem, swd[0].n)
                s.wait_ge(swd[1].sem, swd[1].n)
                sdw.inc(s.dma_start(out=wd0b[:, :], in_=g["wdt_d"][0, :, :]), 16)
            @block.vector
            def _(v):
                v.wait_ge(sdw.sem, sdw.n)
                sv.inc(v.memset(g["nops"][:, 0:1], 0.0))
            dump(nc, g, g["dbg"]["wdt0"], wd0b[:, :], F32)

def phase_p2c(nc, g):
    block = g["block"]
    sv, sa, stn = g["sv"], g["sa"], g["stn"]
    smc = [g["new_sem"]("smc0"), g["new_sem"]("smc1")]
    swd2 = [g["new_sem"]("swd2_0"), g["new_sem"]("swd2_1")]
    sout = g["new_sem"]("sout")
    g["sout"] = sout

    def ln_apply(v, out_ap, in_ap, bnr, bnagg, rs2, n=H, add_ap=None):
        for i in range(n // 512):
            sv.inc(v.bn_stats(out=bnr[:, i*6:(i+1)*6], in_=in_ap[:, i*512:(i+1)*512]))
        _wait(v, sv)
        sv.inc(v.bn_aggr(out=bnagg[:, :], in_=bnr[:, 0:(n // 512) * 6]))
        _wait(v, sv)
        sv.inc(v.tensor_scalar(out=rs2[:, :], in0=bnagg[:, 1:2], scalar1=EPS,
                               scalar2=None, op0=ALU.add))

    with nc.sbuf_tensor("mob", [128, H], F32) as mob, \
         nc.sbuf_tensor("xrb", [128, H], F32) as xrb, \
         nc.sbuf_tensor("x2q", [128, 4 * H], F32) as x2q, \
         nc.sbuf_tensor("ffnq", [128, 4 * H], BF16) as ffnq, \
         nc.sbuf_tensor("x3tq", [128, NKT * 512], BF16) as x3tq, \
         nc.sbuf_tensor("x3n", [128, H], BF16) as x3n, \
         nc.sbuf_tensor("wdb", [128, 2, 8 * 512], BF16) as wdb, \
         nc.sbuf_tensor("bnr2", [128, 96], F32) as bnr2, \
         nc.sbuf_tensor("bnag2", [128, 2], F32) as bnag2, \
         nc.sbuf_tensor("rs2", [128, 1], F32) as rs2, \
         nc.sbuf_tensor("obuf", [128, H], F32) as obuf, \
         nc.psum_tensor("pF", [128, 4 * 512], F32) as pF, \
         nc.psum_tensor("pT3", [128, 512], BF16) as pT3:

        ffn_mm = {}
        for q in range(4):
            x3_done = {}
            for cc in range(4):
                c = q * 4 + cc
                slot = c % 2

                @block.sync
                def _(s, c=c, slot=slot, q=q, cc=cc):
                    if c < 2:
                        _wait(s, sv, sa, stn)
                        _dma_barrier(s, g)
                    else:
                        _wait(s, sv)
                    smc[0].inc(s.dma_start(out=mob[:, :], in_=g["mo_d"][c, :, :]), 16)
                    smc[0].inc(s.dma_start(out=xrb[:, :],
                                           in_=g["xseg"][3 + c*128:3 + (c+1)*128, :]), 16)

                @block.vector
                def _(v, c=c, cc=cc, slot=slot):
                    v.wait_ge(smc[0].sem, smc[0].n)
                    _wait(v, sa)
                    ln_apply(v, None, mob[:, :], bnr2, bnag2, rs2)

                @block.scalar
                def _(sc):
                    _wait(sc, sv)
                    sa.inc(sc.activation(out=rs2[:, :], in_=rs2[:, :], func=AF.Ln))
                    _wait(sc, sa)
                    sa.inc(sc.activation(out=rs2[:, :], in_=rs2[:, :], func=AF.Exp,
                                         scale=-0.5))
                @block.vector
                def _(v, c=c, cc=cc, slot=slot):
                    _wait(v, sa)
                    sv.inc(v.tensor_scalar(out=mob[:, :], in0=mob[:, :],
                                           scalar1=bnag2[:, 0:1], scalar2=rs2[:, 0:1],
                                           op0=ALU.subtract, op1=ALU.mult))
                    _wait(v, sv)
                    sv.inc(v.tensor_tensor(out=x2q[:, cc*H:(cc+1)*H], in0=mob[:, :],
                                           in1=xrb[:, :], op=ALU.add))
                    _wait(v, sv)
                    ln_apply(v, None, x2q[:, cc*H:(cc+1)*H], bnr2, bnag2, rs2)

                @block.scalar
                def _(sc):
                    _wait(sc, sv)
                    sa.inc(sc.activation(out=rs2[:, :], in_=rs2[:, :], func=AF.Ln))
                    _wait(sc, sa)
                    sa.inc(sc.activation(out=rs2[:, :], in_=rs2[:, :], func=AF.Exp,
                                         scale=-0.5))
                @block.vector
                def _(v, c=c, cc=cc):
                    _wait(v, sa)
                    sv.inc(v.tensor_scalar(out=x3n[:, :], in0=x2q[:, cc*H:(cc+1)*H],
                                           scalar1=bnag2[:, 0:1], scalar2=rs2[:, 0:1],
                                           op0=ALU.subtract, op1=ALU.mult))

                for kg in range(8):
                    @block.tensor
                    def _(t, kg=kg, cc=cc):
                        _wait(t, sv)
                        for kk in range(4):
                            k = kg * 4 + kk
                            ins = t.transpose(pT3[:, kk*128:(kk+1)*128],
                                              x3n[:, k*128:(k+1)*128], g["idb"][:, :])
                        stn.inc(ins)

                    @block.vector
                    def _(v, kg=kg, cc=cc):
                        _wait(v, stn)
                        for kk in range(4):
                            k = kg * 4 + kk
                            sv.inc(v.tensor_copy(
                                out=x3tq[:, k*512 + cc*128:k*512 + (cc+1)*128],
                                in_=pT3[:, kk*128:(kk+1)*128]))
                x3_done[cc] = sv.n

            KB2 = 8
            for nb in range(8):
                for kq in range(NKT // KB2):
                    slot = kq % 2
                    @block.gpsimd
                    def _(gp, q=q, nb=nb, kq=kq, slot=slot):
                        key = (q * 8 + nb) * (NKT // KB2) + kq
                        if key < 2:
                            _wait(gp, sv, stn)
                            gp.wait_ge(g["swd"][0].sem, g["swd"][0].n)
                            gp.wait_ge(g["swd"][1].sem, g["swd"][1].n)
                        else:
                            gp.wait_ge(stn.sem, ffn_mm[key - 2])
                        srcw = g["wdt_d"][kq*KB2:(kq+1)*KB2, :, nb*512:(nb+1)*512]
                        swd2[slot].inc(gp.dma_start(
                            out=wdb[:, slot, :].rearrange("p (kk c) -> p kk c", kk=KB2),
                            in_=srcw.transpose([1, 0, 2])), 16)

                    @block.tensor
                    def _(t, q=q, nb=nb, kq=kq, slot=slot):
                        t.wait_ge(swd2[slot].sem, swd2[slot].n)
                        t.wait_ge(sv.sem, x3_done[3])
                        if kq == 0:
                            _wait(t, sa)
                        for kk in range(KB2):
                            k = kq * KB2 + kk
                            for cc in range(4):
                                ins = t.matmul(pF[:, cc*512:(cc+1)*512],
                                               x3tq[:, k*512 + cc*128:k*512 + (cc+1)*128],
                                               wdb[:, slot, kk*512:(kk+1)*512],
                                               start=(k == 0), stop=(k == NKT - 1))
                        stn.inc(ins)
                        ffn_mm[(q * 8 + nb) * (NKT // KB2) + kq] = stn.n

                for cc in range(4):
                    @block.scalar
                    def _(sc, q=q, nb=nb, cc=cc):
                        _wait(sc, stn)
                        sa.inc(sc.activation(out=ffnq[:, cc*H + nb*512:cc*H + (nb+1)*512],
                                             in_=pF[:, cc*512:(cc+1)*512], func=AF.Copy))

            for cc in range(4):
                c = q * 4 + cc
                @block.vector
                def _(v, c=c, cc=cc):
                    _wait(v, sv, sa)
                    if c > 0:
                        v.wait_ge(sout.sem, sout.n)
                    ln_apply(v, None, ffnq[:, cc*H:(cc+1)*H], bnr2, bnag2, rs2)

                @block.scalar
                def _(sc):
                    _wait(sc, sv)
                    sa.inc(sc.activation(out=rs2[:, :], in_=rs2[:, :], func=AF.Ln))
                    _wait(sc, sa)
                    sa.inc(sc.activation(out=rs2[:, :], in_=rs2[:, :], func=AF.Exp,
                                         scale=-0.5))
                @block.vector
                def _(v, c=c, cc=cc):
                    _wait(v, sa)
                    sv.inc(v.tensor_scalar(out=obuf[:, :],
                                           in0=ffnq[:, cc*H:(cc+1)*H],
                                           scalar1=bnag2[:, 0:1], scalar2=rs2[:, 0:1],
                                           op0=ALU.subtract, op1=ALU.mult))
                    _wait(v, sv)
                    sv.inc(v.tensor_tensor(out=obuf[:, :], in0=obuf[:, :],
                                           in1=x2q[:, cc*H:(cc+1)*H], op=ALU.add))

                @block.sync
                def _(s, c=c):
                    _wait(s, sv)
                    sout.inc(s.dma_start(out=g["out"][c*128:(c+1)*128, :],
                                         in_=obuf[:, :]), 16)


def phase_final(nc, g):
    block = g["block"]

    @block.sync
    def _(s):
        _wait(s, g["sv"], g["sa"], g["stn"])
        s.wait_ge(g["sout"].sem, g["sout"].n)


def make_consts():
    i = np.arange(128)
    triu = (i[:, None] <= i[None, :]).astype(ml_dtypes.bfloat16)
    ltri = (i[:, None] > i[None, :]).astype(ml_dtypes.bfloat16)
    trif = (i[:, None] <= i[None, :]).astype(np.float32)
    idb = np.eye(128, dtype=ml_dtypes.bfloat16)
    idf = np.eye(128, dtype=np.float32)
    return triu, ltri, trif, idb, idf


def make_in_maps(inputs):
    x = np.ascontiguousarray(np.asarray(inputs["L0_attn_hidden_states"]))
    w_in = np.ascontiguousarray(np.asarray(inputs["L0_attn_W_in"]))
    cw = np.ascontiguousarray(
        np.asarray(inputs["L0_attn_conv_w"]).reshape(66, 128, 4).transpose(1, 0, 2).reshape(128, 264))
    dtb = np.ascontiguousarray(np.asarray(inputs["L0_attn_dt_bias"]))
    alog = np.ascontiguousarray(np.asarray(inputs["L0_attn_A_log"]))
    dpar = np.ascontiguousarray(np.asarray(inputs["L0_attn_D_param"]))
    nw = np.ascontiguousarray(np.asarray(inputs["L0_attn_norm_weight"]).reshape(NCT_X, 128))
    w_out = np.ascontiguousarray(np.asarray(inputs["L0_attn_W_out"]))
    fpk = np.ascontiguousarray(np.asarray(inputs["L0_ffn_weight_packed"]))
    fsc = np.ascontiguousarray(np.asarray(inputs["L0_ffn_scales"]))
    triu, ltri, trif, idb, idf = make_consts()

    in_maps = []
    for core in range(8):
        b, seg = divmod(core, 4)
        s0 = seg * T
        xs = np.zeros((TEXT, H), dtype=np.float32)
        xs[3:] = x[b, s0:s0 + T]
        if seg > 0:
            xs[0:3] = x[b, s0 - 3:s0]
        segmask = np.broadcast_to((np.arange(4) < seg).astype(np.float32), (128, 4)).copy()
        in_maps.append({
            "xseg": xs, "w_in": w_in, "cw": cw, "dtb": dtb, "alog": alog,
            "dpar": dpar, "nw": nw, "w_out": w_out, "fpk": fpk, "fsc": fsc,
            "triu": triu, "ltri": ltri, "trif": trif, "idb": idb, "idf": idf,
            "eyef": idf, "segm": segmask,
        })
    return in_maps


def kernel(**inputs):
    nc = build_program()
    in_maps = make_in_maps(inputs)
    res = run_bass_kernel_spmd(nc, in_maps, core_ids=list(range(8)))
    out = np.empty((2, 8192, H), dtype=np.float32)
    for core in range(8):
        b, seg = divmod(core, 4)
        out[b, seg * T:(seg + 1) * T] = np.asarray(res.results[core]["out"])
    return out


# version 09
# speedup vs baseline: 1.1218x; 1.1218x over previous
import numpy as np
import ml_dtypes
from contextlib import ExitStack

import concourse.bass as bass
import concourse.mybir as mybir
from concourse.bass_utils import run_bass_kernel_spmd

F32 = mybir.dt.float32
BF16 = mybir.dt.bfloat16
F16 = mybir.dt.float16
I32 = mybir.dt.int32
AF = mybir.ActivationFunctionType
ALU = mybir.AluOpType

H = 4096
DSSM = 8192
NH = 64
PD = 128
ND = 128
CONV = DSSM + 2 * ND
PROJ = DSSM + CONV + NH
EPS = 1e-5
T = 2048
L = 128
NCH = T // L
NKT = H // 128
NCT_X = DSSM // 128
TEXT = T + 3

DEBUG = ()
SIM_SKIP_CC = False


class Ctr:
    def __init__(self, sem):
        self.sem = sem
        self.n = 0

    def inc(self, instr, k=1):
        instr.then_inc(self.sem, k)
        self.n += k
        return self.n


def _wait(eng, *ctrs):
    for c in ctrs:
        if c.n > 0:
            eng.wait_ge(c.sem, c.n)


def _dma_barrier(eng, g):
    for c in g["dma_ctrs"]:
        if c.n > 0:
            eng.wait_ge(c.sem, c.n)


def build_program():
    nc = bass.Bass("TRN2", target_bir_lowering=False)

    xseg = nc.dram_tensor("xseg", [TEXT, H], F32, kind="ExternalInput").ap()
    w_in = nc.dram_tensor("w_in", [H, PROJ], F32, kind="ExternalInput").ap()
    cw = nc.dram_tensor("cw", [128, 66 * 4], F32, kind="ExternalInput").ap()
    dtb_i = nc.dram_tensor("dtb", [NH], F32, kind="ExternalInput").ap()
    alog_i = nc.dram_tensor("alog", [NH], F32, kind="ExternalInput").ap()
    dpar_i = nc.dram_tensor("dpar", [NH], F32, kind="ExternalInput").ap()
    nw_i = nc.dram_tensor("nw", [NCT_X, 128], F32, kind="ExternalInput").ap()
    w_out = nc.dram_tensor("w_out", [DSSM, H], F32, kind="ExternalInput").ap()
    fpk = nc.dram_tensor("fpk", [H, 2048], I32, kind="ExternalInput").ap()
    fsc = nc.dram_tensor("fsc", [H, 128], F16, kind="ExternalInput").ap()
    triu_i = nc.dram_tensor("triu", [128, 128], BF16, kind="ExternalInput").ap()
    ltri_i = nc.dram_tensor("ltri", [128, 128], BF16, kind="ExternalInput").ap()
    trif_i = nc.dram_tensor("trif", [128, 128], F32, kind="ExternalInput").ap()
    idb_i = nc.dram_tensor("idb", [128, 128], BF16, kind="ExternalInput").ap()
    idf_i = nc.dram_tensor("idf", [128, 128], F32, kind="ExternalInput").ap()
    eyef_i = nc.dram_tensor("eyef", [128, 128], F32, kind="ExternalInput").ap()
    segm_i = nc.dram_tensor("segm", [128, 4], F32, kind="ExternalInput").ap()

    out = nc.dram_tensor("out", [T, H], F32, kind="ExternalOutput").ap()

    u_d = nc.dram_tensor("u_d", [NCT_X, NCH, 128, 128], BF16, kind="Internal").ap()
    xfm_d = nc.dram_tensor("xfm_d", [NCT_X, 128, T], BF16, kind="Internal").ap()
    sz_d = nc.dram_tensor("sz_d", [NCT_X, 128, T], BF16, kind="Internal").ap()
    yg_d = nc.dram_tensor("yg_d", [NCT_X, NCH, 128, 128], BF16, kind="Internal").ap()
    mo_d = nc.dram_tensor("mo_d", [NCH, 128, H], F32, kind="Internal").ap()
    wdt_d = nc.dram_tensor("wdt_d", [NKT, 128, H], BF16, kind="Internal").ap()
    cc_in = nc.dram_tensor("cc_in", [128, DSSM + NH], F32, kind="Internal").ap()
    cc_out = nc.dram_tensor("cc_out", [512, DSSM + NH], F32, kind="Internal").ap()

    dbg = {}
    for name in DEBUG:
        shp = {
            "xlnt0": [128, TEXT], "dtfm": [NH, T], "S": [128, NH], "rdt": [128, NH],
            "drp": [128, NH], "bfm": [128, T], "cfm": [128, T], "xfm0": [128, T],
            "sz0": [128, T], "u00": [128, 128], "hloc": [128, DSSM], "hin": [128, DSSM],
            "ych0": [128, DSSM], "yg0": [128, DSSM], "rstd0": [128, 1],
            "mo0": [128, H], "x20": [128, H], "ffn0": [128, H], "wdt0": [128, H],
        }[name]
        dbg[name] = nc.dram_tensor("dbg_" + name, shp, F32, kind="ExternalOutput").ap()

    prog(nc, locals())
    return nc


def prog(nc, io):
    g = dict(io)
    st = ExitStack()
    E = st.enter_context

    dtb = E(nc.sbuf_tensor("dtb_s", [NH, 1], F32))
    expA = E(nc.sbuf_tensor("expA_s", [NH, 1], F32))
    negA = E(nc.sbuf_tensor("negA_s", [NH, 1], F32))
    drep64 = E(nc.sbuf_tensor("drep64", [128, NH], F32))
    nwT = E(nc.sbuf_tensor("nwT", [128, NCT_X], F32))
    triu = E(nc.sbuf_tensor("triu_s", [128, 128], BF16))
    ltri = E(nc.sbuf_tensor("ltri_s", [128, 128], BF16))
    trif = E(nc.sbuf_tensor("trif_s", [128, 128], F32))
    idb = E(nc.sbuf_tensor("idb_s", [128, 128], BF16))
    idf = E(nc.sbuf_tensor("idf_s", [128, 128], F32))
    eyef = E(nc.sbuf_tensor("eyef_s", [128, 128], F32))
    segm = E(nc.sbuf_tensor("segm_s", [128, 4], F32))
    onesb = E(nc.sbuf_tensor("onesb", [128, 1], BF16))
    nops = E(nc.sbuf_tensor("nops", [1, 4], F32))
    nwrows = E(nc.sbuf_tensor("nwrows", [NCT_X, 128], F32))
    cwt = E(nc.sbuf_tensor("cwt", [128, 66 * 4], F32))
    rstd_all = E(nc.sbuf_tensor("rstd_all", [128, NCH], F32))

    stn = Ctr(E(nc.semaphore("stn")))
    sv = Ctr(E(nc.semaphore("sv")))
    sa = Ctr(E(nc.semaphore("sa")))
    scl = Ctr(E(nc.semaphore("scl")))
    sone = Ctr(E(nc.semaphore("sone")))
    spl = Ctr(E(nc.semaphore("spl")))
    dma_ctrs = [sone]
    def new_sem(name):
        c = Ctr(E(nc.semaphore(name)))
        dma_ctrs.append(c)
        return c
    g_newsem = new_sem
    block = E(nc.Block())
    g.update(locals())
    g["new_sem"] = new_sem

    st25 = ExitStack()
    for _nm, _shp, _dt in (("dtt", [128, NCH * NH], F32), ("rdt", [128, NCH * NH], F32),
                           ("drp", [128, NCH * NH], F32), ("S_all", [128, NCH * NH], F32),
                           ("dseg", [128, NH], F32), ("rdts", [128, NCH * NH], F32),
                           ("bfm", [128, T], BF16),
                           ("cfm", [128, T], BF16), ("btt", [128, NCH * 128], BF16)):
        g[_nm] = st25.enter_context(nc.sbuf_tensor(_nm, _shp, _dt))
    phase_params(nc, g)
    st2 = ExitStack()
    g["xlnT"] = st2.enter_context(nc.sbuf_tensor("xlnT", [128, NKT * TEXT], BF16))
    phase_p1a(nc, g)
    phase_p1b(nc, g)
    phase_p1c(nc, g)
    st2.close()
    st3 = ExitStack()
    g["hT"] = st3.enter_context(nc.sbuf_tensor("hT", [128, DSSM], F32))
    phase_p1d(nc, g)
    phase_exch(nc, g)
    phase_p2a(nc, g)
    st3.close()
    st25.close()
    phase_p2b(nc, g)
    phase_dequant(nc, g)
    phase_p2c(nc, g)
    phase_final(nc, g)
    st.close()


def phase_params(nc, g):
    block, sv, sa, stn, sone = g["block"], g["sv"], g["sa"], g["stn"], g["sone"]

    @block.sync
    def _(s):
        sone.inc(s.dma_start(out=g["dtb"][:, 0], in_=g["dtb_i"][:]), 16)
        sone.inc(s.dma_start(out=g["expA"][:, 0], in_=g["alog_i"][:]), 16)
        sone.inc(s.dma_start(out=g["triu"][:], in_=g["triu_i"][:]), 16)
        sone.inc(s.dma_start(out=g["ltri"][:], in_=g["ltri_i"][:]), 16)
        sone.inc(s.dma_start(out=g["trif"][:], in_=g["trif_i"][:]), 16)
        sone.inc(s.dma_start(out=g["idb"][:], in_=g["idb_i"][:]), 16)
        sone.inc(s.dma_start(out=g["idf"][:], in_=g["idf_i"][:]), 16)
        sone.inc(s.dma_start(out=g["eyef"][:], in_=g["eyef_i"][:]), 16)
        sone.inc(s.dma_start(out=g["segm"][:, :], in_=g["segm_i"][:, :]), 16)
        sone.inc(s.dma_start(out=g["drep64"][0:1, :], in_=g["dpar_i"][:].unsqueeze(0)), 16)
        sone.inc(s.dma_start(out=g["nwrows"][:, :], in_=g["nw_i"][:, :]), 16)
        sone.inc(s.dma_start(out=g["cwt"][:, :], in_=g["cw"][:, :]), 16)
    g["SONE_TOTAL"] = sone.n

    @block.scalar
    def _(sc):
        sc.wait_ge(sone.sem, g["SONE_TOTAL"])
        sa.inc(sc.activation(out=g["expA"][:], in_=g["expA"][:], func=AF.Exp))

    @block.vector
    def _(v):
        _wait(v, sa)
        sv.inc(v.tensor_scalar(out=g["negA"][:], in0=g["expA"][:], scalar1=-1.0,
                               scalar2=None, op0=ALU.mult))
        sv.inc(v.memset(g["onesb"][:], 1.0))

    with nc.psum_tensor("pp", [128, NH], F32) as pp, \
         nc.psum_tensor("pnw", [128, NCT_X], F32) as pnw, \
         nc.sbuf_tensor("onesf", [1, 128], F32) as onesf:
        @block.vector
        def _(v):
            sv.inc(v.memset(onesf[:], 1.0))

        @block.tensor
        def _(t):
            _wait(t, sv)
            t.wait_ge(sone.sem, g["SONE_TOTAL"])
            stn.inc(t.matmul(pp[:, :], onesf[0:1, :], g["drep64"][0:1, :], start=True, stop=True))
            stn.inc(t.transpose(pnw[:, :], g["nwrows"][:, :], g["idf"][0:NCT_X, 0:NCT_X]))

        @block.vector
        def _(v):
            _wait(v, stn)
            sv.inc(v.tensor_copy(out=g["drep64"][:], in_=pp[:]))
            sv.inc(v.tensor_copy(out=g["nwT"][:], in_=pnw[:]))


def phase_p1a(nc, g):
    block = g["block"]
    sv, sa, stn = g["sv"], g["sa"], g["stn"]
    xlnT = g["xlnT"]
    sx = [g["new_sem"]("sx0"), g["new_sem"]("sx1")]

    with nc.sbuf_tensor("xch", [128, 2, H], F32) as xch, \
         nc.sbuf_tensor("xlnb", [128, H], BF16) as xlnb, \
         nc.sbuf_tensor("bnr", [128, 6 * 8], F32) as bnr, \
         nc.sbuf_tensor("bnagg", [128, 2], F32) as bnagg, \
         nc.sbuf_tensor("rs", [128, 1], F32) as rs, \
         nc.psum_tensor("ptr", [128, 2048], BF16) as ptr:

        free_gate = []

        def ln_rows(it, nrows, src_rows, dst_cols):
            slot = it % 2
            @block.sync
            def _(s):
                if it >= 2:
                    s.wait_ge(sv.sem, free_gate[it - 2])
                sx[slot].inc(s.dma_start(out=xch[0:nrows, slot, :],
                                         in_=g["xseg"][src_rows, :]), 16)

            @block.vector
            def _(v):
                v.wait_ge(sx[slot].sem, sx[slot].n)
                _wait(v, stn)
                for i in range(8):
                    sv.inc(v.bn_stats(out=bnr[0:nrows, i*6:(i+1)*6],
                                      in_=xch[0:nrows, slot, i*512:(i+1)*512]))
                _wait(v, sv)
                sv.inc(v.bn_aggr(out=bnagg[0:nrows, :], in_=bnr[0:nrows, :]))
                _wait(v, sv)
                sv.inc(v.tensor_scalar(out=rs[0:nrows, :], in0=bnagg[0:nrows, 1:2],
                                       scalar1=EPS, scalar2=None, op0=ALU.add))

            @block.scalar
            def _(sc):
                _wait(sc, sv)
                sa.inc(sc.activation(out=rs[0:nrows, :], in_=rs[0:nrows, :], func=AF.Ln))
                _wait(sc, sa)
                sa.inc(sc.activation(out=rs[0:nrows, :], in_=rs[0:nrows, :], func=AF.Exp,
                                     scale=-0.5))

            @block.vector
            def _(v):
                _wait(v, sa)
                sv.inc(v.tensor_scalar(out=xlnb[0:nrows, :], in0=xch[0:nrows, slot, :],
                                       scalar1=bnagg[0:nrows, 0:1], scalar2=rs[0:nrows, 0:1],
                                       op0=ALU.subtract, op1=ALU.mult))
            free_gate.append(sv.n)

            evac_done = {}
            for grp in range(NKT // 4):
                gp2 = grp % 2
                @block.tensor
                def _(t, grp=grp, gp2=gp2):
                    if grp == 0:
                        _wait(t, sv)
                    if grp >= 2:
                        t.wait_ge(sv.sem, evac_done[grp - 2])
                    for kk in range(4):
                        k = grp * 4 + kk
                        ins = t.transpose(ptr[:, gp2*1024 + kk*128: gp2*1024 + kk*128 + nrows],
                                          xlnb[0:nrows, k*128:(k+1)*128],
                                          g["idb"][0:nrows, 0:nrows])
                    stn.inc(ins)

                @block.vector
                def _(v, grp=grp, gp2=gp2):
                    _wait(v, stn)
                    for kk in range(4):
                        k = grp * 4 + kk
                        sv.inc(v.tensor_copy(
                            out=xlnT[:, k*TEXT + dst_cols : k*TEXT + dst_cols + nrows],
                            in_=ptr[:, gp2*1024 + kk*128: gp2*1024 + kk*128 + nrows]))
                    evac_done[grp] = sv.n

        ln_rows(0, 3, slice(0, 3), 0)
        for c in range(NCH):
            ln_rows(1 + c, 128, slice(3 + c * 128, 3 + (c + 1) * 128), 3 + c * 128)

    if "xlnt0" in g["dbg"]:
        dump(nc, g, g["dbg"]["xlnt0"], xlnT[:, 0:TEXT], F32)


_DUMP_N = [0]

def dump(nc, g, dst, src_ap, dt):
    block, sv = g["block"], g["sv"]
    P, Fn = src_ap.shape[0], int(np.prod(src_ap.shape[1:]))
    _DUMP_N[0] += 1
    sd = g["new_sem"]("sdmp%d" % _DUMP_N[0])
    CH = min(Fn, 1024)
    src2 = src_ap if len(src_ap.shape) == 2 else src_ap.rearrange("p a b -> p (a b)")
    with nc.sbuf_tensor("dmp%d" % _DUMP_N[0], [128, CH], F32) as dmp:
        for of in range(0, Fn, CH):
            n = min(CH, Fn - of)
            @block.vector
            def _(v, of=of, n=n):
                _wait(v, g["stn"], g["sa"], sv)
                _dma_barrier(v, g)
                v.wait_ge(sd.sem, sd.n)
                sv.inc(v.tensor_copy(out=dmp[0:P, 0:n], in_=src2[0:P, of:of+n]))

            @block.sync
            def _(s, of=of, n=n):
                _wait(s, sv)
                d2 = dst if len(dst.shape) == 2 else dst.unsqueeze(0)
                sd.inc(s.dma_start(out=d2[0:P, of:of+n], in_=dmp[0:P, 0:n]), 16)
            s_last = None

        @block.sync
        def _(s):
            s.wait_ge(sd.sem, sd.n)

        @block.vector
        def _(v):
            v.wait_ge(sd.sem, sd.n)
            sv.inc(v.memset(dmp[0:1, 0:1], 0.0))


def phase_p1b(nc, g):
    block = g["block"]
    sv, sa, stn = g["sv"], g["sa"], g["stn"]
    xlnT = g["xlnT"]
    s1b = g["new_sem"]("s1b")

    with nc.sbuf_tensor("wdtb", [128, NKT * NH], BF16) as wdtb, \
         nc.sbuf_tensor("dtfm", [NH, T], F32) as dtfm, \
         nc.sbuf_tensor("spt", [NH, T], F32) as spt, \
         nc.sbuf_tensor("ldafm", [NH, T], F32) as ldafm, \
         nc.sbuf_tensor("ldab", [128, NH], BF16) as ldab, \
         nc.sbuf_tensor("Rt", [128, NH], F32) as Rt, \
         nc.sbuf_tensor("rr", [128, NH], F32) as rr, \
         nc.psum_tensor("pdt", [NH, T], F32) as pdt, \
         nc.psum_tensor("ptt", [128, 256], F32) as ptt, \
         nc.psum_tensor("pcs", [128, 2 * NH], F32) as pcs:

        @block.gpsimd
        def _(gp):
            _wait(gp, g["sv"], g["stn"])
            for kq in range(4):
                src = g["w_in"][:, PROJ - NH:PROJ].rearrange(
                    "(kt p) c -> p kt c", p=128)[:, kq*8:(kq+1)*8, :]
                s1b.inc(gp.dma_start(out=wdtb.ap().rearrange(
                    "p (kt c) -> p kt c", c=NH)[:, kq*8:(kq+1)*8, :], in_=src), 16)
        S1B_TOTAL = s1b.n

        @block.tensor
        def _(t):
            t.wait_ge(s1b.sem, S1B_TOTAL)
            _wait(t, sv, stn)
            for k in range(NKT):
                for tb in range(4):
                    ins = t.matmul(pdt[:, tb*512:(tb+1)*512],
                                   wdtb[:, k*NH:(k+1)*NH],
                                   xlnT[:, k*TEXT + 3 + tb*512 : k*TEXT + 3 + (tb+1)*512],
                                   start=(k == 0), stop=(k == NKT - 1))
            stn.inc(ins)

        @block.vector
        def _(v):
            _wait(v, stn)
            v.wait_ge(g["sone"].sem, g["SONE_TOTAL"])
            sv.inc(v.tensor_scalar(out=dtfm[:], in0=pdt[:], scalar1=g["dtb"][0:NH, 0:1],
                                   scalar2=None, op0=ALU.add))

        if "dtfm" in g["dbg"]:
            dump(nc, g, g["dbg"]["dtfm"], dtfm[:, :], F32)

        @block.scalar
        def _(sc):
            _wait(sc, sv)
            sa.inc(sc.activation(out=spt[:], in_=dtfm[:], func=AF.Abs))
            _wait(sc, sa)
            sa.inc(sc.activation(out=spt[:], in_=spt[:], func=AF.Exp, scale=-1.0))
            _wait(sc, sa)
            sa.inc(sc.activation(out=spt[:], in_=spt[:], func=AF.Ln, bias=1.0))

        @block.vector
        def _(v):
            _wait(v, sa)
            sv.inc(v.tensor_scalar(out=dtfm[:], in0=dtfm[:], scalar1=0.0,
                                   scalar2=None, op0=ALU.max))
            _wait(v, sv)
            sv.inc(v.tensor_tensor(out=dtfm[:], in0=dtfm[:], in1=spt[:], op=ALU.add))
            _wait(v, sv)
            sv.inc(v.tensor_scalar(out=ldafm[:], in0=dtfm[:], scalar1=g["negA"][0:NH, 0:1],
                                   scalar2=None, op0=ALU.mult))

        for c in range(NCH):
            @block.tensor
            def _(t, c=c):
                _wait(t, sv)
                stn.inc(t.transpose(ptt[:, 0:64], dtfm[0:NH, c*128:(c+1)*128], g["idf"][0:NH, 0:NH]))
                stn.inc(t.transpose(ptt[:, 64:128], ldafm[0:NH, c*128:(c+1)*128], g["idf"][0:NH, 0:NH]))

            @block.vector
            def _(v, c=c):
                _wait(v, stn)
                sv.inc(v.tensor_copy(out=g["dtt"][:, c*NH:(c+1)*NH], in_=ptt[:, 0:64]))
                sv.inc(v.tensor_copy(out=ldab[:], in_=ptt[:, 64:128]))

            @block.tensor
            def _(t, c=c):
                _wait(t, sv)
                stn.inc(t.matmul(pcs[:, 0:NH], g["triu"][:], ldab[:], start=True, stop=True))
                stn.inc(t.matmul(pcs[:, NH:2*NH], g["ltri"][:], ldab[:], start=True, stop=True))

            @block.vector
            def _(v, c=c):
                _wait(v, stn)
                sv.inc(v.tensor_copy(out=g["S_all"][:, c*NH:(c+1)*NH], in_=pcs[:, 0:NH]))
                sv.inc(v.tensor_copy(out=Rt[:], in_=pcs[:, NH:2*NH]))
                _wait(v, sv)
                sv.inc(v.tensor_tensor(out=g["drp"][:, c*NH:(c+1)*NH],
                                       in0=g["S_all"][:, c*NH:(c+1)*NH], in1=Rt[:], op=ALU.add))

            @block.scalar
            def _(sc, c=c):
                _wait(sc, sv)
                sa.inc(sc.activation(out=g["drp"][:, c*NH:(c+1)*NH],
                                     in_=g["drp"][:, c*NH:(c+1)*NH], func=AF.Exp))
                sa.inc(sc.activation(out=rr[:], in_=Rt[:], func=AF.Exp))

            @block.vector
            def _(v, c=c):
                _wait(v, sa)
                sv.inc(v.tensor_tensor(out=g["rdt"][:, c*NH:(c+1)*NH], in0=rr[:],
                                       in1=g["dtt"][:, c*NH:(c+1)*NH], op=ALU.mult))

        @block.vector
        def _(v):
            _wait(v, sv, g["sa"])
            sv.inc(v.memset(g["rdts"][:, (NCH-1)*NH:NCH*NH], 1.0))
            for c in range(NCH - 2, -1, -1):
                _wait(v, sv)
                sv.inc(v.tensor_tensor(out=g["rdts"][:, c*NH:(c+1)*NH],
                                       in0=g["rdts"][:, (c+1)*NH:(c+2)*NH],
                                       in1=g["drp"][:, (c+1)*NH:(c+2)*NH], op=ALU.mult))
            _wait(v, sv)
            sv.inc(v.tensor_tensor(out=g["dseg"][:], in0=g["rdts"][:, 0:NH],
                                   in1=g["drp"][:, 0:NH], op=ALU.mult))
            _wait(v, sv)
            for c in range(NCH):
                sv.inc(v.tensor_tensor(out=g["rdts"][:, c*NH:(c+1)*NH],
                                       in0=g["rdts"][:, c*NH:(c+1)*NH],
                                       in1=g["rdt"][:, c*NH:(c+1)*NH], op=ALU.mult))

    for nm, src2 in (("S", g["S_all"]), ("rdt", g["rdt"]), ("drp", g["drp"])):
        if nm in g["dbg"]:
            dump(nc, g, g["dbg"][nm], src2[:, 0:NH], F32)


def phase_p1c(nc, g):
    block = g["block"]
    sv, sa, stn = g["sv"], g["sa"], g["stn"]
    xlnT = g["xlnT"]
    swb = [g["new_sem"]("swb0"), g["new_sem"]("swb1")]
    ssp = [g["new_sem"]("ssp0"), g["new_sem"]("ssp1")]

    order = [("B", 0, DSSM + DSSM), ("C", 0, DSSM + DSSM + ND)]
    order += [("x", i, DSSM + 128 * i) for i in range(NCT_X)]
    order += [("z", j, 128 * j) for j in range(64)]

    with nc.sbuf_tensor("wbuf", [128, 2, NKT * 128], BF16) as wbuf, \
         nc.sbuf_tensor("raw", [128, 2051], F32) as raw, \
         nc.sbuf_tensor("acc", [128, 2048], F32) as acc, \
         nc.sbuf_tensor("stage", [128, 2, 2048], BF16) as stage, \
         nc.psum_tensor("pA", [128, 2048], F32) as pA, \
         nc.psum_tensor("pH", [128, 512], F32) as pH, \
         nc.psum_tensor("ptb", [128, 128], BF16) as ptb:

        mm_done = {}
        conv_done = {}
        cd_sem = {}
        silu_done = {}

        for i, (kind, idx, c0) in enumerate(order):
            slot = i % 2
            is_xbc = kind in ("B", "C", "x")
            cd_sem[i] = "sa" if is_xbc else "sv"

            @block.gpsimd
            def _(gp, i=i, slot=slot, c0=c0):
                if i < 2:
                    _wait(gp, sv, stn)
                if i >= 2:
                    gp.wait_ge(stn.sem, mm_done[i - 2])
                for q in range(4):
                    srcw = g["w_in"][:, c0:c0 + 128].rearrange(
                        "(kt p) c -> p kt c", p=128)[:, q*8:(q+1)*8, :]
                    swb[slot].inc(gp.dma_start(
                        out=wbuf[:, slot, :].rearrange("p (kt c) -> p kt c", c=128)
                        [:, q*8:(q+1)*8, :], in_=srcw), 16)

            @block.tensor
            def _(t, i=i, slot=slot, is_xbc=is_xbc):
                t.wait_ge(swb[slot].sem, swb[slot].n)
                if i == 0:
                    _wait(t, sv, sa)
                if i >= 1:
                    if cd_sem[i - 1] == "sa":
                        t.wait_ge(sa.sem, conv_done[i - 1])
                    else:
                        t.wait_ge(sv.sem, conv_done[i - 1])
                for k in range(NKT):
                    lhs = wbuf[:, slot, k*128:(k+1)*128]
                    base = k * TEXT + (0 if is_xbc else 3)
                    for tb in range(4):
                        ins = t.matmul(pA[:, tb*512:(tb+1)*512], lhs,
                                       xlnT[:, base + tb*512: base + (tb+1)*512],
                                       start=(k == 0), stop=(k == NKT - 1))
                    if is_xbc:
                        ins = t.matmul(pH[:, 0:3], lhs,
                                       xlnT[:, k*TEXT + 2048: k*TEXT + 2051],
                                       start=(k == 0), stop=(k == NKT - 1))
                stn.inc(ins)
                mm_done[i] = stn.n

            if is_xbc:
                @block.scalar
                def _(sc, i=i):
                    _wait(sc, stn)
                    _wait(sc, sv)
                    sa.inc(sc.activation(out=raw[:, 0:2048], in_=pA[:, 0:2048], func=AF.Copy))
                    sa.inc(sc.activation(out=raw[:, 2048:2051], in_=pH[:, 0:3], func=AF.Copy))
                    conv_done[i] = None
                silu_psum_free = sa.n

                @block.vector
                def _(v, i=i, idx=idx, kind=kind, spf=silu_psum_free):
                    v.wait_ge(sa.sem, spf)
                    _wait(v, sa)
                    cwc = g["cwt"]
                    ctg = {"x": idx, "B": NCT_X, "C": NCT_X + 1}[kind]
                    sv.inc(v.tensor_scalar(out=acc[:, :], in0=raw[:, 0:2048],
                                           scalar1=cwc[:, 4*ctg:4*ctg+1],
                                           scalar2=None, op0=ALU.mult))
                    _wait(v, sv)
                    for j in (1, 2, 3):
                        sv.inc(v.scalar_tensor_tensor(
                            out=acc[:, 0:2048], in0=raw[:, j:2048 + j],
                            scalar=cwc[:, 4*ctg+j:4*ctg+j+1],
                            in1=acc[:, 0:2048], op0=ALU.mult, op1=ALU.add))
                        _wait(v, sv)
                conv_done[i] = silu_psum_free

                if kind == "x":
                    @block.scalar
                    def _(sc, i=i, slot=slot):
                        _wait(sc, sv)
                        if i >= 2:
                            sc.wait_ge(ssp[slot].sem, ssp[slot].n)
                        sa.inc(sc.activation(out=stage[:, slot, :], in_=acc[:, :],
                                             func=AF.Silu))
                        silu_done[i] = sa.n
                else:
                    dest = g["bfm"] if kind == "B" else g["cfm"]
                    @block.scalar
                    def _(sc, i=i, dest=dest):
                        _wait(sc, sv)
                        sa.inc(sc.activation(out=dest[:, :], in_=acc[:, :], func=AF.Silu))
                        silu_done[i] = sa.n
            else:
                @block.scalar
                def _(sc, i=i, slot=slot):
                    _wait(sc, stn)
                    if i >= 2:
                        sc.wait_ge(ssp[slot].sem, ssp[slot].n)
                    sa.inc(sc.activation(out=stage[:, slot, :], in_=pA[:, :], func=AF.Silu))
                    silu_done[i] = sa.n

                @block.vector
                def _(v, i=i):
                    _wait(v, sa)
                    sv.inc(v.memset(g["nops"][:, 0:1], 0.0))
                    conv_done[i] = sv.n

            if kind == "B":
                for c in range(NCH):
                    @block.tensor
                    def _(t, c=c, i=i):
                        t.wait_ge(sa.sem, silu_done[i])
                        _wait(t, sv)
                        stn.inc(t.transpose(ptb[:, :], g["bfm"][:, c*128:(c+1)*128],
                                            g["idb"][:, :]))

                    @block.vector
                    def _(v, c=c):
                        _wait(v, stn)
                        sv.inc(v.tensor_copy(out=g["btt"][:, c*128:(c+1)*128], in_=ptb[:, :]))

            if kind in ("x", "z"):
                @block.sync
                def _(s, i=i, slot=slot, idx=idx, kind=kind):
                    s.wait_ge(sa.sem, silu_done[i])
                    dst = g["xfm_d"][idx, :, :] if kind == "x" else g["sz_d"][idx, :, :]
                    ssp[slot].inc(s.dma_start(out=dst, in_=stage[:, slot, :]), 16)
                    if kind == "x":
                        g.setdefault("xspill", {})[idx] = (slot, ssp[slot].n)

        g["SSP_FIN"] = (ssp[0].n, ssp[1].n)
        g["ssp"] = ssp

    for nm, ctsel in (("bfm", None), ("cfm", None)):
        if nm in g["dbg"]:
            dump(nc, g, g["dbg"][nm], g[nm][:, :], F32)

def phase_p1d(nc, g):
    block = g["block"]
    sv, sa, stn = g["sv"], g["sa"], g["stn"]
    sxf = [g["new_sem"]("sxf0"), g["new_sem"]("sxf1")]
    sus = [g["new_sem"]("sus0"), g["new_sem"]("sus1")]
    g["sus"] = sus

    with nc.sbuf_tensor("xfb", [128, 2, T], BF16) as xfb, \
         nc.sbuf_tensor("ust", [128, 2, NCH * 128], BF16) as ust, \
         nc.sbuf_tensor("bdt16", [128, NCH * 128], BF16) as bdt16, \
         nc.psum_tensor("ptu", [128, 512], BF16) as ptu, \
         nc.psum_tensor("psS", [128, 1024], F32) as psS:

        @block.vector
        def _(v):
            _wait(v, sa, stn)
            sv.inc(v.memset(g["hT"][:, :], 0.0))

        tr_done = {}
        sp_gate = {}

        for ct in range(NCT_X):
            slot = ct % 2

            @block.sync
            def _(s, ct=ct, slot=slot):
                spslot, spn = g["xspill"][ct]
                s.wait_ge(g["ssp"][spslot].sem, spn)
                if ct < 2:
                    _wait(s, sv)
                else:
                    s.wait_ge(stn.sem, sp_gate[ct - 2])
                sxf[slot].inc(s.dma_start(out=xfb[:, slot, :],
                                          in_=g["xfm_d"][ct, :, :]), 16)

            for cg in range(4):
                @block.tensor
                def _(t, ct=ct, slot=slot, cg=cg):
                    t.wait_ge(sxf[slot].sem, sxf[slot].n)
                    _wait(t, sv, sa)
                    for cc in range(4):
                        c = cg * 4 + cc
                        ins = t.transpose(ptu[:, cc*128:(cc+1)*128],
                                          xfb[:, slot, c*128:(c+1)*128], g["idb"][:, :])
                    stn.inc(ins)

                @block.vector
                def _(v, ct=ct, slot=slot, cg=cg):
                    _wait(v, stn)
                    if cg == 0 and ct >= 2:
                        v.wait_ge(sus[slot].sem, sus[slot].n)
                    for cc in range(4):
                        c = cg * 4 + cc
                        sv.inc(v.tensor_copy(out=ust[:, slot, c*128:(c+1)*128],
                                             in_=ptu[:, cc*128:(cc+1)*128]))

            tr_done[ct] = sv.n

            @block.sync
            def _(s, ct=ct, slot=slot):
                s.wait_ge(sv.sem, tr_done[ct])
                sus[slot].inc(s.dma_start(
                    out=g["u_d"][ct, :, :, :].transpose([1, 0, 2]),
                    in_=ust[:, slot, :].rearrange("p (c q) -> p c q", q=128)), 16)

            hpar = ct % 2
            @block.vector
            def _(v, ct=ct):
                _wait(v, stn)
                for c in range(NCH):
                    sv.inc(v.tensor_scalar(
                        out=bdt16[:, c*128:(c+1)*128], in0=g["btt"][:, c*128:(c+1)*128],
                        scalar1=g["rdts"][:, c*NH + ct:c*NH + ct + 1],
                        scalar2=None, op0=ALU.mult))
            bd_done = sv.n

            @block.tensor
            def _(t, ct=ct, slot=slot, bd=bd_done, hpar=hpar):
                t.wait_ge(sv.sem, bd)
                _wait(t, sa)
                for c in range(NCH):
                    ins = t.matmul(psS[:, hpar*512:hpar*512+128],
                                   bdt16[:, c*128:(c+1)*128],
                                   ust[:, slot, c*128:(c+1)*128],
                                   start=(c == 0), stop=(c == NCH - 1))
                stn.inc(ins)
                sp_gate[ct] = stn.n

            @block.scalar
            def _(sc, ct=ct, hpar=hpar):
                _wait(sc, stn)
                sa.inc(sc.activation(out=g["hT"][:, ct*128:(ct+1)*128],
                                     in_=psS[:, hpar*512:hpar*512+128], func=AF.Copy))

    if "u00" in g["dbg"]:
        with nc.sbuf_tensor("u00b", [128, 128], BF16) as u00b:
            sdu = g["new_sem"]("sdu")
            @block.sync
            def _(s):
                _wait(s, sv)
                sdu.inc(s.dma_start(out=u00b[:, :], in_=g["u_d"][0, 0, :, :]), 16)
            @block.vector
            def _(v):
                v.wait_ge(sdu.sem, sdu.n)
                sv.inc(v.memset(g["nops"][:, 0:1], 0.0))
            dump(nc, g, g["dbg"]["u00"], u00b[:, :], F32)
    if "sz0" in g["dbg"]:
        with nc.sbuf_tensor("sz0b", [128, T], BF16) as sz0b:
            sdz = g["new_sem"]("sdz")
            @block.sync
            def _(s):
                _wait(s, sv)
                s.wait_ge(g["ssp"][0].sem, g["SSP_FIN"][0])
                s.wait_ge(g["ssp"][1].sem, g["SSP_FIN"][1])
                sdz.inc(s.dma_start(out=sz0b[:, :], in_=g["sz_d"][0, :, :]), 16)
            @block.vector
            def _(v):
                v.wait_ge(sdz.sem, sdz.n)
                sv.inc(v.memset(g["nops"][:, 0:1], 0.0))
            dump(nc, g, g["dbg"]["sz0"], sz0b[:, :], F32)
    if "hloc" in g["dbg"]:
        dump(nc, g, g["dbg"]["hloc"], g["hT"][:, :], F32)

def phase_exch(nc, g):
    block = g["block"]
    sv, sa, stn, scl = g["sv"], g["sa"], g["stn"], g["scl"]
    sxc = g["new_sem"]("sxc")
    sgh = g["new_sem"]("sgh")

    with nc.sbuf_tensor("gh", [128, DSSM + NH], F32) as gh, \
         nc.sbuf_tensor("dsg4", [128, 4 * NH], F32) as dsg4, \
         nc.sbuf_tensor("fl", [128, 3 * NH], F32) as fl, \
         nc.sbuf_tensor("cf", [128, 4 * NH], F32) as cf, \
         nc.sbuf_tensor("oms", [128, 4], F32) as oms:

        @block.sync
        def _(s):
            _wait(s, sv, g["sa"], g["stn"])
            _dma_barrier(s, g)
            sxc.inc(s.dma_start(out=g["cc_in"][:, 0:DSSM], in_=g["hT"][:, :]), 16)
            sxc.inc(s.dma_start(out=g["cc_in"][:, DSSM:], in_=g["dseg"][:, :]), 16)
        CC_STAGE = sxc.n

        @block.gpsimd
        def _(gp):
            gp.wait_ge(sxc.sem, CC_STAGE)
            if SIM_SKIP_CC:
                for l in range(4):
                    scl.inc(gp.dma_start(out=g["cc_out"][l*128:(l+1)*128, :],
                                         in_=g["cc_in"][:, :]), 16)
            else:
                scl.inc(gp.collective_compute(
                    "AllGather", ALU.bypass,
                    ins=[g["cc_in"][:, :].opt()], outs=[g["cc_out"][:, :].opt()],
                    replica_groups=[[0, 1, 2, 3], [4, 5, 6, 7]],
                ), 1)

        @block.sync
        def _(s):
            s.wait_ge(scl.sem, scl.n if not SIM_SKIP_CC else scl.n)
            for l in range(4):
                sxc.inc(s.dma_start(out=dsg4[:, l*NH:(l+1)*NH],
                                    in_=g["cc_out"][l*128:(l+1)*128, DSSM:]), 16)
        CC_D = sxc.n

        @block.vector
        def _(v):
            v.wait_ge(sxc.sem, CC_D)
            sv.inc(v.tensor_scalar(out=oms[:, :], in0=g["segm"][:, :], scalar1=-1.0,
                                   scalar2=1.0, op0=ALU.mult, op1=ALU.add))
            _wait(v, sv)
            for l in (1, 2, 3):
                sv.inc(v.tensor_scalar(out=fl[:, (l-1)*NH:l*NH], in0=dsg4[:, l*NH:(l+1)*NH],
                                       scalar1=g["segm"][:, l:l+1], scalar2=oms[:, l:l+1],
                                       op0=ALU.mult, op1=ALU.add))
            _wait(v, sv)
            sv.inc(v.memset(cf[:, 3*NH:4*NH], 1.0))
            _wait(v, sv)
            sv.inc(v.tensor_copy(out=cf[:, 2*NH:3*NH], in_=fl[:, 2*NH:3*NH]))
            _wait(v, sv)
            sv.inc(v.tensor_tensor(out=cf[:, 1*NH:2*NH], in0=fl[:, 1*NH:2*NH],
                                   in1=cf[:, 2*NH:3*NH], op=ALU.mult))
            _wait(v, sv)
            sv.inc(v.tensor_tensor(out=cf[:, 0:NH], in0=fl[:, 0:NH],
                                   in1=cf[:, 1*NH:2*NH], op=ALU.mult))
            _wait(v, sv)
            for j in range(4):
                sv.inc(v.tensor_scalar(out=cf[:, j*NH:(j+1)*NH], in0=cf[:, j*NH:(j+1)*NH],
                                       scalar1=g["segm"][:, j:j+1], scalar2=None,
                                       op0=ALU.mult))
            _wait(v, sv)
            sv.inc(v.memset(g["hT"][:, :], 0.0))

        for j in range(4):
            @block.sync
            def _(s, j=j):
                _wait(s, sv)
                sgh.inc(s.dma_start(out=gh[:, 0:DSSM],
                                    in_=g["cc_out"][j*128:(j+1)*128, 0:DSSM]), 16)

            @block.vector
            def _(v, j=j):
                v.wait_ge(sgh.sem, sgh.n)
                for h in range(NH):
                    _wait(v, sv)
                    sv.inc(v.scalar_tensor_tensor(
                        out=g["hT"][:, h*128:(h+1)*128], in0=gh[:, h*128:(h+1)*128],
                        scalar=cf[:, j*NH + h:j*NH + h + 1],
                        in1=g["hT"][:, h*128:(h+1)*128], op0=ALU.mult, op1=ALU.add))

    if "hin" in g["dbg"]:
        dump(nc, g, g["dbg"]["hin"], g["hT"][:, :], F32)


def phase_p2a(nc, g):
    block = g["block"]
    sv, sa, stn = g["sv"], g["sa"], g["stn"]
    su2 = [g["new_sem"]("su2_0"), g["new_sem"]("su2_1")]
    ssz2 = g["new_sem"]("ssz2")
    sfl = g["new_sem"]("sfl")
    sygw = g["new_sem"]("sygw")
    g["sygw"] = sygw

    from contextlib import ExitStack as _ES
    with _ES() as _stk:
        _E = _stk.enter_context
        ub = _E(nc.sbuf_tensor("ub", [128, DSSM], BF16))
        szb = _E(nc.sbuf_tensor("szb", [128, DSSM], BF16))
        mf = _E(nc.sbuf_tensor("mf", [128, DSSM], BF16))
        ccd = _E(nc.sbuf_tensor("ccd", [128, DSSM], BF16))
        hTb = _E(nc.sbuf_tensor("hTb", [128, DSSM], BF16))
        pairA = _E(nc.sbuf_tensor("pairA", [2, 4096], F32))
        pairB = _E(nc.sbuf_tensor("pairB", [2, 4096], F32))
        gmt = _E(nc.sbuf_tensor("gmt", [128, 128], BF16))
        stb = _E(nc.sbuf_tensor("stb", [NH, 128], F32))
        steb = _E(nc.sbuf_tensor("steb", [NH, 128], BF16))
        cdf = _E(nc.sbuf_tensor("cdf", [1, 2048], BF16))
        onesrow = _E(nc.sbuf_tensor("onesrow", [1, 128], BF16))
        dbig = _E(nc.sbuf_tensor("dbig", [128, 4096], BF16))
        dbig2 = _E(nc.sbuf_tensor("dbig2", [128, 4096], BF16))
        bdt8 = _E(nc.sbuf_tensor("bdt8", [128, 1024], BF16))
        ssrow = _E(nc.sbuf_tensor("ssrow", [1, 128], F32))
        rstd_r = _E(nc.sbuf_tensor("rstd_r", [1, 128], F32))
        pAll = _E(nc.psum_tensor("pAll", [128, 4096], F32))
        pG = pAll[:, 0:128]
        pST = pAll[0:NH, 128:256]
        pRow = pAll[0:1, 3584:3712]
        pCol = pAll[:, 3712:3720]

        sflP = g["new_sem"]("sflP")
        @block.vector
        def _(v):
            _wait(v, sv)
            sv.inc(v.memset(pairA[0:1, :], 1.0))
            sv.inc(v.memset(onesrow[:, :], 1.0))
            sv.inc(v.memset(stb[:, :], -1.0))
        @block.gpsimd
        def _(gp):
            _wait(gp, sv, stn, sa)
            sflP.inc(gp.dma_start(out=pairB[1:2, :], in_=stb[0:32, :]), 16)
        INIT_FL = sflP.n

        ych_free = {}
        ub_inflight = {}

        for c in range(NCH):
            par = c % 2
            @block.sync
            def _(s, c=c, par=par):
                if c == 0:
                    _wait(s, sv, stn, sa)
                else:
                    s.wait_ge(stn.sem, ub_inflight[c - 1])
                su2[par].inc(s.dma_start(out=ub[:, :],
                                         in_=g["u_d"][:, c, :, :].transpose([1, 0, 2])), 16)
                for q in range(4):
                    ssz2.inc(s.dma_start(
                        out=szb.ap().rearrange("p (ct t) -> p ct t", t=128)[:, q*16:(q+1)*16, :],
                        in_=g["sz_d"][q*16:(q+1)*16, :, c*128:(c+1)*128].transpose([1, 0, 2])), 16)

            @block.tensor
            def _(t, c=c):
                _wait(t, sv, sa)
                if c == 0:
                    t.wait_ge(sflP.sem, INIT_FL)
                stn.inc(t.matmul(pG, g["bfm"][:, c*128:(c+1)*128],
                                 g["cfm"][:, c*128:(c+1)*128], start=True, stop=True))
                stn.inc(t.transpose(pST, g["S_all"][:, c*NH:(c+1)*NH],
                                    g["idf"][0:128, 0:128]))

            @block.vector
            def _(v, c=c):
                _wait(v, stn)
                sv.inc(v.tensor_tensor(out=gmt[:, :], in0=pG, in1=g["trif"][:, :],
                                       op=ALU.mult))
                sv.inc(v.tensor_copy(out=stb[:, :], in_=pST))

            @block.scalar
            def _(sc, c=c):
                _wait(sc, sv)
                sa.inc(sc.activation(out=steb[:, :], in_=stb[:, :], func=AF.Exp))

            dmm_done = {}
            for half in range(2):
                @block.gpsimd
                def _(gp, c=c, half=half):
                    _wait(gp, sv)
                    if half == 1:
                        gp.wait_ge(stn.sem, dmm_done[0])
                    sflP.inc(gp.dma_start(out=pairA[1:2, :],
                                          in_=stb[half*32:half*32+32, :]), 16)
                    sflP.inc(gp.dma_start(out=pairB[0:1, :],
                                          in_=stb[half*32:half*32+32, :]), 16)

                @block.tensor
                def _(t, c=c, half=half):
                    t.wait_ge(sflP.sem, sflP.n)
                    if half == 0:
                        _wait(t, sv)
                    else:
                        _wait(t, sv)
                    for hl in range(32):
                        ins = t.matmul(pAll[:, hl*128:(hl+1)*128],
                                       pairA[0:2, hl*128:(hl+1)*128],
                                       pairB[0:2, hl*128:(hl+1)*128],
                                       start=True, stop=True)
                    stn.inc(ins)
                    dmm_done[half] = stn.n

                @block.vector
                def _(v, c=c, half=half):
                    _wait(v, stn, sa)
                    sv.inc(v.tensor_scalar(out=dbig[:, :], in0=pAll[:, 0:4096],
                                           scalar1=0.0, scalar2=None, op0=ALU.min))

                @block.scalar
                def _(sc, c=c, half=half):
                    _wait(sc, sv)
                    if half == 0 and c > 0:
                        sc.wait_ge(sygw.sem, sygw.n)
                    sa.inc(sc.activation(out=dbig[:, :], in_=dbig[:, :], func=AF.Exp))
                    _wait(sc, sa)
                    for hl in range(32):
                        h = half * 32 + hl
                        sa.inc(sc.activation(out=mf[:, (half*32+hl)*128:(half*32+hl+1)*128],
                                             in_=dbig[:, hl*128:(hl+1)*128],
                                             func=AF.Copy,
                                             scale=g["dtt"][:, c*NH + h:c*NH + h + 1]))

                @block.gpsimd
                def _(gp, c=c, half=half):
                    _wait(gp, sa)
                    for hl in range(32):
                        h = half * 32 + hl
                        g["spl"].inc(gp.tensor_tensor(out=mf[:, h*128:(h+1)*128],
                                                      in0=mf[:, h*128:(h+1)*128],
                                                      in1=gmt[:, :], op=ALU.mult))

                @block.vector
                def _(v, c=c, half=half):
                    _wait(v, g["spl"])
                    for hl in range(32):
                        h = half * 32 + hl
                        sv.inc(v.scalar_tensor_tensor(
                            out=mf[:, h*128:(h+1)*128], in0=g["eyef"][:, :],
                            scalar=g["drep64"][:, h:h+1],
                            in1=mf[:, h*128:(h+1)*128], op0=ALU.mult, op1=ALU.add))

            for rnd in range(4):
                h0 = rnd * 16
                rpar = rnd % 2
                @block.gpsimd
                def _(gp, c=c, h0=h0):
                    _wait(gp, sa, stn)
                    sfl.inc(gp.dma_start(out=cdf[0:1, :], in_=steb[h0:h0+16, :]), 16)

                @block.tensor
                def _(t, rpar=rpar, c=c, rnd=rnd):
                    t.wait_ge(sfl.sem, sfl.n)
                    _wait(t, sv)
                    for qq in range(4):
                        stn.inc(t.matmul(pAll[:, rpar*2048 + qq*512:rpar*2048 + (qq+1)*512],
                                         onesrow[0:1, :], cdf[0:1, qq*512:(qq+1)*512],
                                         start=True, stop=True))

                @block.vector
                def _(v, c=c, h0=h0, rpar=rpar):
                    _wait(v, stn)
                    for hh in range(16):
                        h = h0 + hh
                        sv.inc(v.tensor_tensor(
                            out=ccd[:, h*128:(h+1)*128],
                            in0=g["cfm"][:, c*128:(c+1)*128],
                            in1=pAll[:, rpar*2048 + hh*128:rpar*2048 + (hh+1)*128],
                            op=ALU.mult))

            @block.vector
            def _(v, c=c):
                _wait(v, sv)
                sv.inc(v.tensor_copy(out=hTb[:, :], in_=g["hT"][:, :]))

            yev = {}
            for grp in range(16):
                h0 = grp * 4
                gpar = grp % 2
                @block.tensor
                def _(t, c=c, h0=h0, gpar=gpar, par=par, grp=grp):
                    if grp == 0:
                        t.wait_ge(su2[par].sem, su2[par].n)
                        _wait(t, sv, sa)
                    if grp >= 2:
                        t.wait_ge(sa.sem, yev[grp - 2])
                    for hh in range(4):
                        h = h0 + hh
                        t.matmul(pAll[:, 2048 + gpar*512 + hh*128: 2048 + gpar*512 + (hh+1)*128],
                                 ub[:, h*128:(h+1)*128], mf[:, h*128:(h+1)*128],
                                 start=True, stop=False)
                        stn.inc(t.matmul(pAll[:, 2048 + gpar*512 + hh*128: 2048 + gpar*512 + (hh+1)*128],
                                         hTb[:, h*128:(h+1)*128], ccd[:, h*128:(h+1)*128],
                                         start=False, stop=True))

                @block.scalar
                def _(sc, h0=h0, gpar=gpar):
                    _wait(sc, stn)
                    sa.inc(sc.activation(out=ccd[:, h0*128:(h0+4)*128],
                                         in_=pAll[:, 2048 + gpar*512:2048 + (gpar+1)*512],
                                         func=AF.Copy))
                    yev[grp] = sa.n

            if c == 0 and "ych0" in g["dbg"]:
                dump(nc, g, g["dbg"]["ych0"], ccd[:, :], F32)

            @block.vector
            def _(v, c=c):
                _wait(v, sv, sa)
                v.wait_ge(ssz2.sem, ssz2.n)
                sv.inc(v.tensor_tensor(out=ccd[:, :], in0=ccd[:, :], in1=szb[:, :],
                                       op=ALU.mult))

            @block.scalar
            def _(sc, c=c):
                _wait(sc, sv)
                sa.inc(sc.activation(out=dbig[:, 0:4096], in_=ccd[:, 0:4096], func=AF.Square))
                sa.inc(sc.activation(out=dbig2[:, 0:4096], in_=ccd[:, 4096:8192], func=AF.Square))

            @block.tensor
            def _(t, c=c):
                _wait(t, sa, sv)
                for ct in range(NCT_X):
                    srcq = dbig if ct < 32 else dbig2
                    ins = t.matmul(pRow, g["onesb"][:, 0:1],
                                   srcq[:, (ct % 32)*128:(ct % 32 + 1)*128],
                                   start=(ct == 0), stop=(ct == NCT_X - 1))
                stn.inc(ins)

            @block.vector
            def _(v, c=c):
                _wait(v, stn)
                sv.inc(v.tensor_scalar(out=ssrow[:, :], in0=pRow,
                                       scalar1=1.0 / DSSM, scalar2=EPS,
                                       op0=ALU.mult, op1=ALU.add))
            @block.scalar
            def _(sc, c=c):
                _wait(sc, sv)
                sa.inc(sc.activation(out=rstd_r[:, :], in_=ssrow[:, :], func=AF.Ln))
                _wait(sc, sa)
                sa.inc(sc.activation(out=rstd_r[:, :], in_=rstd_r[:, :], func=AF.Exp,
                                     scale=-0.5))
            @block.tensor
            def _(t, c=c):
                _wait(t, sa, sv)
                stn.inc(t.transpose(pCol[:, 0:1], rstd_r[0:1, :], g["idf"][0:1, 0:1]))
            @block.vector
            def _(v, c=c):
                _wait(v, stn)
                sv.inc(v.tensor_copy(out=g["rstd_all"][:, c:c+1], in_=pCol[:, 0:1]))
                for ct in range(NCT_X):
                    sv.inc(v.tensor_scalar(out=mf[:, ct*128:(ct+1)*128],
                                           in0=ccd[:, ct*128:(ct+1)*128],
                                           scalar1=g["nwT"][:, ct:ct+1],
                                           scalar2=None, op0=ALU.mult))
                ych_free[c] = sv.n

            if c == 0 and "yg0" in g["dbg"]:
                dump(nc, g, g["dbg"]["yg0"], mf[:, :], F32)
            if c == 0 and "rstd0" in g["dbg"]:
                dump(nc, g, g["dbg"]["rstd0"], g["rstd_all"][:, 0:1], F32)

            @block.sync
            def _(s, c=c):
                s.wait_ge(sv.sem, ych_free[c])
                for q in range(4):
                    sygw.inc(s.dma_start(
                        out=g["yg_d"][q*16:(q+1)*16, c, :, :].transpose([1, 0, 2]),
                        in_=mf.ap().rearrange("p (ct t) -> p ct t", t=128)
                        [:, q*16:(q+1)*16, :]), 16)

            rec2 = {}
            for rnd in range(8):
                @block.vector
                def _(v, c=c, rnd=rnd):
                    _wait(v, stn)
                    for hh in range(8):
                        h = rnd * 8 + hh
                        sv.inc(v.tensor_scalar(
                            out=bdt8[:, hh*128:(hh+1)*128],
                            in0=g["btt"][:, c*128:(c+1)*128],
                            scalar1=g["rdt"][:, c*NH + h:c*NH + h + 1],
                            scalar2=None, op0=ALU.mult))
                bd8 = sv.n

                for gq in range(2):
                    gpar2 = (rnd * 2 + gq) % 2
                    @block.tensor
                    def _(t, c=c, rnd=rnd, gq=gq, gpar2=gpar2, bd8=bd8):
                        if gq == 0:
                            t.wait_ge(sv.sem, bd8)
                        key = rnd * 2 + gq
                        if key >= 2:
                            t.wait_ge(sv.sem, rec2[key - 2])
                        for hh in range(4):
                            hq = gq * 4 + hh
                            h = rnd * 8 + hq
                            ins = t.matmul(pAll[:, gpar2*512 + hh*128:gpar2*512 + (hh+1)*128],
                                           bdt8[:, hq*128:(hq+1)*128],
                                           ub[:, h*128:(h+1)*128], start=True, stop=True)
                        stn.inc(ins)
                        if rnd == 7 and gq == 1:
                            ub_inflight[c] = stn.n

                    @block.vector
                    def _(v, c=c, rnd=rnd, gq=gq, gpar2=gpar2):
                        _wait(v, stn)
                        for hh in range(4):
                            h = rnd * 8 + gq * 4 + hh
                            sv.inc(v.scalar_tensor_tensor(
                                out=g["hT"][:, h*128:(h+1)*128],
                                in0=g["hT"][:, h*128:(h+1)*128],
                                scalar=g["drp"][:, c*NH + h:c*NH + h + 1],
                                in1=pAll[:, gpar2*512 + hh*128:gpar2*512 + (hh+1)*128],
                                op0=ALU.mult, op1=ALU.add))
                        rec2[rnd * 2 + gq] = sv.n

    g["YGW_FIN"] = sygw.n


def phase_p2b(nc, g):
    block = g["block"]
    sv, sa, stn = g["sv"], g["sa"], g["stn"]
    sygr = g["new_sem"]("sygr")
    swo = [g["new_sem"]("swo0"), g["new_sem"]("swo1")]
    smo = [g["new_sem"]("smo0"), g["new_sem"]("smo1")]
    NKT2 = DSSM // 128

    with nc.sbuf_tensor("ygb", [128, NKT2 * 1024], BF16) as ygb, \
         nc.sbuf_tensor("wob", [128, 2, 4 * 512], BF16) as wob, \
         nc.sbuf_tensor("mstage", [128, 2, 512], F32) as mstage, \
         nc.psum_tensor("pW", [128, 8 * 512], F32) as pW:

        mm_k = {}
        for oct_ in range(2):
            @block.sync
            def _(s, oct_=oct_):
                if oct_ == 0:
                    _wait(s, sv, sa, stn)
                    _dma_barrier(s, g)
                else:
                    _wait(s, sv)
                    _wait(s, stn)
                for ct in range(NKT2):
                    sygr.inc(s.dma_start(
                        out=ygb[:, ct*1024:(ct+1)*1024].rearrange("p (c t) -> p c t", t=128),
                        in_=g["yg_d"][ct, oct_*8:(oct_+1)*8, :, :].transpose([1, 0, 2])), 16)

            KB = 4
            for jb in range(8):
                for kq in range(NKT2 // KB):
                    slot = kq % 2
                    @block.gpsimd
                    def _(gp, oct_=oct_, jb=jb, kq=kq, slot=slot):
                        if oct_ == 0 and jb == 0 and kq < 2:
                            _wait(gp, sv, stn)
                        key = (jb + oct_ * 8) * (NKT2 // KB) + kq
                        if key >= 2:
                            gp.wait_ge(stn.sem, mm_k[key - 2])
                        srcw = g["w_out"][kq*KB*128:(kq+1)*KB*128, jb*512:(jb+1)*512]
                        swo[slot].inc(gp.dma_start(
                            out=wob[:, slot, :].rearrange("p (kk c) -> p kk c", kk=KB),
                            in_=srcw.rearrange("(kk p) c -> p kk c", p=128)), 16)

                    @block.tensor
                    def _(t, oct_=oct_, jb=jb, kq=kq, slot=slot):
                        t.wait_ge(swo[slot].sem, swo[slot].n)
                        t.wait_ge(sygr.sem, sygr.n)
                        if kq == 0:
                            _wait(t, sa)
                        for kk in range(KB):
                            k = kq * KB + kk
                            for cc in range(8):
                                ins = t.matmul(pW[:, cc*512:(cc+1)*512],
                                               ygb[:, k*1024 + cc*128:k*1024 + (cc+1)*128],
                                               wob[:, slot, kk*512:(kk+1)*512],
                                               start=(k == 0), stop=(k == NKT2 - 1))
                        stn.inc(ins)
                        mm_k[(jb + oct_ * 8) * (NKT2 // KB) + kq] = stn.n

                for cc in range(8):
                    eslot = cc % 2
                    @block.scalar
                    def _(sc, oct_=oct_, jb=jb, cc=cc, eslot=eslot):
                        _wait(sc, stn)
                        if not (oct_ == 0 and jb == 0 and cc < 2):
                            sc.wait_ge(smo[eslot].sem, smo[eslot].n)
                        c = oct_ * 8 + cc
                        sa.inc(sc.activation(out=mstage[:, eslot, :], in_=pW[:, cc*512:(cc+1)*512],
                                             func=AF.Copy, scale=g["rstd_all"][:, c:c+1]))

                    @block.sync
                    def _(s, oct_=oct_, jb=jb, cc=cc, eslot=eslot):
                        _wait(s, sa)
                        c = oct_ * 8 + cc
                        smo[eslot].inc(s.dma_start(out=g["mo_d"][c, :, jb*512:(jb+1)*512],
                                                   in_=mstage[:, eslot, :]), 16)

    g["SMO_FIN"] = (smo[0], smo[1])
    if "mo0" in g["dbg"]:
        with nc.sbuf_tensor("mo0b", [128, H], F32) as mo0b:
            sdm = g["new_sem"]("sdm")
            @block.sync
            def _(s):
                s.wait_ge(smo[0].sem, smo[0].n)
                s.wait_ge(smo[1].sem, smo[1].n)
                sdm.inc(s.dma_start(out=mo0b[:, :], in_=g["mo_d"][0, :, :]), 16)
            @block.vector
            def _(v):
                v.wait_ge(sdm.sem, sdm.n)
                sv.inc(v.memset(g["nops"][:, 0:1], 0.0))
            dump(nc, g, g["dbg"]["mo0"], mo0b[:, :], F32)

def phase_dequant(nc, g):
    block = g["block"]
    sv, sa, stn = g["sv"], g["sa"], g["stn"]
    spk = [g["new_sem"]("spk0"), g["new_sem"]("spk1")]
    swd = [g["new_sem"]("swd0"), g["new_sem"]("swd1")]
    g["swd"] = swd

    with nc.sbuf_tensor("pkb", [128, 2, 2048], I32) as pkb, \
         nc.sbuf_tensor("scb", [128, 2, 128], F16) as scb, \
         nc.sbuf_tensor("srep", [128, 2048], F32) as srep, \
         nc.sbuf_tensor("loint", [128, 2048], I32) as loint, \
         nc.sbuf_tensor("hiint", [128, 2048], I32) as hiint, \
         nc.sbuf_tensor("wqt", [128, H], BF16) as wqt, \
         nc.sbuf_tensor("wdst", [128, 2, NKT * 128], BF16) as wdst, \
         nc.psum_tensor("pT2", [128, 2048], BF16) as pT2:

        tr_done = {}
        for nt in range(NKT):
            slot = nt % 2
            @block.sync
            def _(s, nt=nt, slot=slot):
                if nt < 2:
                    _wait(s, sv, sa, stn)
                    _dma_barrier(s, g)
                else:
                    s.wait_ge(sv.sem, tr_done[nt - 2])
                spk[slot].inc(s.dma_start(out=pkb[:, slot, :],
                                          in_=g["fpk"][nt*128:(nt+1)*128, :]), 16)
                spk[slot].inc(s.dma_start(out=scb[:, slot, :],
                                          in_=g["fsc"][nt*128:(nt+1)*128, :]), 16)

            @block.vector
            def _(v, nt=nt, slot=slot):
                v.wait_ge(spk[slot].sem, spk[slot].n)
                _wait(v, stn)
                for i in range(16):
                    sv.inc(v.tensor_copy(out=srep.ap().rearrange("p (b i) -> p b i", i=16)[:, :, i],
                                         in_=scb[:, slot, :]))
                sv.inc(v.tensor_scalar(out=loint[:, :], in0=pkb[:, slot, :], scalar1=15,
                                       scalar2=None, op0=ALU.bitwise_and))
                sv.inc(v.tensor_scalar(out=hiint[:, :], in0=pkb[:, slot, :], scalar1=4,
                                       scalar2=15, op0=ALU.arith_shift_right,
                                       op1=ALU.bitwise_and))
                _wait(v, sv)
                sv.inc(v.scalar_tensor_tensor(out=wqt[:, 0:H:2], in0=loint[:, :], scalar=-8.0,
                                              in1=srep[:, :], op0=ALU.add, op1=ALU.mult))
                sv.inc(v.scalar_tensor_tensor(out=wqt[:, 1:H:2], in0=hiint[:, :], scalar=-8.0,
                                              in1=srep[:, :], op0=ALU.add, op1=ALU.mult))

            dqe = {}
            for kg in range(8):
                kp = kg % 2
                @block.tensor
                def _(t, nt=nt, kg=kg, kp=kp, slot=slot):
                    if kg == 0:
                        _wait(t, sv)
                    if kg >= 2:
                        t.wait_ge(sv.sem, dqe[kg - 2])
                    for kk in range(4):
                        ks = kg * 4 + kk
                        ins = t.transpose(pT2[:, kp*1024 + kk*128: kp*1024 + (kk+1)*128],
                                          wqt[:, ks*128:(ks+1)*128], g["idb"][:, :])
                    stn.inc(ins)

                @block.vector
                def _(v, nt=nt, kg=kg, kp=kp, slot=slot):
                    _wait(v, stn)
                    if kg == 0 and nt >= 2:
                        v.wait_ge(swd[slot].sem, swd[slot].n)
                    for kk in range(4):
                        ks = kg * 4 + kk
                        sv.inc(v.tensor_copy(out=wdst[:, slot, ks*128:(ks+1)*128],
                                             in_=pT2[:, kp*1024 + kk*128: kp*1024 + (kk+1)*128]))
                    dqe[kg] = sv.n
            tr_done[nt] = sv.n

            @block.sync
            def _(s, nt=nt, slot=slot):
                s.wait_ge(sv.sem, tr_done[nt])
                swd[slot].inc(s.dma_start(
                    out=g["wdt_d"][:, :, nt*128:(nt+1)*128].transpose([1, 0, 2]),
                    in_=wdst[:, slot, :].rearrange("p (ks n) -> p ks n", n=128)), 16)

    if "wdt0" in g["dbg"]:
        with nc.sbuf_tensor("wd0b", [128, H], BF16) as wd0b:
            sdw = g["new_sem"]("sdw")
            @block.sync
            def _(s):
                s.wait_ge(swd[0].sem, swd[0].n)
                s.wait_ge(swd[1].sem, swd[1].n)
                sdw.inc(s.dma_start(out=wd0b[:, :], in_=g["wdt_d"][0, :, :]), 16)
            @block.vector
            def _(v):
                v.wait_ge(sdw.sem, sdw.n)
                sv.inc(v.memset(g["nops"][:, 0:1], 0.0))
            dump(nc, g, g["dbg"]["wdt0"], wd0b[:, :], F32)

def phase_p2c(nc, g):
    block = g["block"]
    sv, sa, stn = g["sv"], g["sa"], g["stn"]
    smc = [g["new_sem"]("smc0"), g["new_sem"]("smc1")]
    swd2 = [g["new_sem"]("swd2_0"), g["new_sem"]("swd2_1")]
    sout = g["new_sem"]("sout")
    g["sout"] = sout

    def ln_apply(v, out_ap, in_ap, bnr, bnagg, rs2, n=H, add_ap=None):
        for i in range(n // 512):
            sv.inc(v.bn_stats(out=bnr[:, i*6:(i+1)*6], in_=in_ap[:, i*512:(i+1)*512]))
        _wait(v, sv)
        sv.inc(v.bn_aggr(out=bnagg[:, :], in_=bnr[:, 0:(n // 512) * 6]))
        _wait(v, sv)
        sv.inc(v.tensor_scalar(out=rs2[:, :], in0=bnagg[:, 1:2], scalar1=EPS,
                               scalar2=None, op0=ALU.add))

    with nc.sbuf_tensor("mob", [128, H], F32) as mob, \
         nc.sbuf_tensor("xrb", [128, H], F32) as xrb, \
         nc.sbuf_tensor("x2q", [128, 4 * H], F32) as x2q, \
         nc.sbuf_tensor("ffnq", [128, 4 * H], BF16) as ffnq, \
         nc.sbuf_tensor("x3tq", [128, NKT * 512], BF16) as x3tq, \
         nc.sbuf_tensor("x3n", [128, H], BF16) as x3n, \
         nc.sbuf_tensor("wdb", [128, 2, 8 * 512], BF16) as wdb, \
         nc.sbuf_tensor("bnr2", [128, 96], F32) as bnr2, \
         nc.sbuf_tensor("bnag2", [128, 2], F32) as bnag2, \
         nc.sbuf_tensor("rs2", [128, 1], F32) as rs2, \
         nc.sbuf_tensor("obuf", [128, H], F32) as obuf, \
         nc.psum_tensor("pF", [128, 4 * 512], F32) as pF, \
         nc.psum_tensor("pT3", [128, 2048], BF16) as pT3:

        ffn_mm = {}
        for q in range(4):
            x3_done = {}
            for cc in range(4):
                c = q * 4 + cc
                slot = c % 2

                @block.sync
                def _(s, c=c, slot=slot, q=q, cc=cc):
                    if c < 2:
                        _wait(s, sv, sa, stn)
                        _dma_barrier(s, g)
                    else:
                        _wait(s, sv)
                    smc[0].inc(s.dma_start(out=mob[:, :], in_=g["mo_d"][c, :, :]), 16)
                    smc[0].inc(s.dma_start(out=xrb[:, :],
                                           in_=g["xseg"][3 + c*128:3 + (c+1)*128, :]), 16)

                @block.vector
                def _(v, c=c, cc=cc, slot=slot):
                    v.wait_ge(smc[0].sem, smc[0].n)
                    _wait(v, sa)
                    ln_apply(v, None, mob[:, :], bnr2, bnag2, rs2)

                @block.scalar
                def _(sc):
                    _wait(sc, sv)
                    sa.inc(sc.activation(out=rs2[:, :], in_=rs2[:, :], func=AF.Ln))
                    _wait(sc, sa)
                    sa.inc(sc.activation(out=rs2[:, :], in_=rs2[:, :], func=AF.Exp,
                                         scale=-0.5))
                @block.vector
                def _(v, c=c, cc=cc, slot=slot):
                    _wait(v, sa)
                    sv.inc(v.tensor_scalar(out=mob[:, :], in0=mob[:, :],
                                           scalar1=bnag2[:, 0:1], scalar2=rs2[:, 0:1],
                                           op0=ALU.subtract, op1=ALU.mult))
                    _wait(v, sv)
                    sv.inc(v.tensor_tensor(out=x2q[:, cc*H:(cc+1)*H], in0=mob[:, :],
                                           in1=xrb[:, :], op=ALU.add))
                    _wait(v, sv)
                    ln_apply(v, None, x2q[:, cc*H:(cc+1)*H], bnr2, bnag2, rs2)

                @block.scalar
                def _(sc):
                    _wait(sc, sv)
                    sa.inc(sc.activation(out=rs2[:, :], in_=rs2[:, :], func=AF.Ln))
                    _wait(sc, sa)
                    sa.inc(sc.activation(out=rs2[:, :], in_=rs2[:, :], func=AF.Exp,
                                         scale=-0.5))
                @block.vector
                def _(v, c=c, cc=cc):
                    _wait(v, sa)
                    sv.inc(v.tensor_scalar(out=x3n[:, :], in0=x2q[:, cc*H:(cc+1)*H],
                                           scalar1=bnag2[:, 0:1], scalar2=rs2[:, 0:1],
                                           op0=ALU.subtract, op1=ALU.mult))

                x3e = {}
                for kg in range(8):
                    kp = kg % 2
                    @block.tensor
                    def _(t, kg=kg, kp=kp, cc=cc):
                        if kg == 0:
                            _wait(t, sv)
                        if kg >= 2:
                            t.wait_ge(sv.sem, x3e[kg - 2])
                        for kk in range(4):
                            k = kg * 4 + kk
                            ins = t.transpose(pT3[:, kp*1024 + kk*128: kp*1024 + (kk+1)*128],
                                              x3n[:, k*128:(k+1)*128], g["idb"][:, :])
                        stn.inc(ins)

                    @block.vector
                    def _(v, kg=kg, kp=kp, cc=cc):
                        _wait(v, stn)
                        for kk in range(4):
                            k = kg * 4 + kk
                            sv.inc(v.tensor_copy(
                                out=x3tq[:, k*512 + cc*128:k*512 + (cc+1)*128],
                                in_=pT3[:, kp*1024 + kk*128: kp*1024 + (kk+1)*128]))
                        x3e[kg] = sv.n
                x3_done[cc] = sv.n

            KB2 = 8
            for nb in range(8):
                for kq in range(NKT // KB2):
                    slot = kq % 2
                    @block.gpsimd
                    def _(gp, q=q, nb=nb, kq=kq, slot=slot):
                        key = (q * 8 + nb) * (NKT // KB2) + kq
                        if key < 2:
                            _wait(gp, sv, stn)
                            gp.wait_ge(g["swd"][0].sem, g["swd"][0].n)
                            gp.wait_ge(g["swd"][1].sem, g["swd"][1].n)
                        else:
                            gp.wait_ge(stn.sem, ffn_mm[key - 2])
                        srcw = g["wdt_d"][kq*KB2:(kq+1)*KB2, :, nb*512:(nb+1)*512]
                        swd2[slot].inc(gp.dma_start(
                            out=wdb[:, slot, :].rearrange("p (kk c) -> p kk c", kk=KB2),
                            in_=srcw.transpose([1, 0, 2])), 16)

                    @block.tensor
                    def _(t, q=q, nb=nb, kq=kq, slot=slot):
                        t.wait_ge(swd2[slot].sem, swd2[slot].n)
                        t.wait_ge(sv.sem, x3_done[3])
                        if kq == 0:
                            _wait(t, sa)
                        for kk in range(KB2):
                            k = kq * KB2 + kk
                            for cc in range(4):
                                ins = t.matmul(pF[:, cc*512:(cc+1)*512],
                                               x3tq[:, k*512 + cc*128:k*512 + (cc+1)*128],
                                               wdb[:, slot, kk*512:(kk+1)*512],
                                               start=(k == 0), stop=(k == NKT - 1))
                        stn.inc(ins)
                        ffn_mm[(q * 8 + nb) * (NKT // KB2) + kq] = stn.n

                for cc in range(4):
                    @block.scalar
                    def _(sc, q=q, nb=nb, cc=cc):
                        _wait(sc, stn)
                        sa.inc(sc.activation(out=ffnq[:, cc*H + nb*512:cc*H + (nb+1)*512],
                                             in_=pF[:, cc*512:(cc+1)*512], func=AF.Copy))

            for cc in range(4):
                c = q * 4 + cc
                @block.vector
                def _(v, c=c, cc=cc):
                    _wait(v, sv, sa)
                    if c > 0:
                        v.wait_ge(sout.sem, sout.n)
                    ln_apply(v, None, ffnq[:, cc*H:(cc+1)*H], bnr2, bnag2, rs2)

                @block.scalar
                def _(sc):
                    _wait(sc, sv)
                    sa.inc(sc.activation(out=rs2[:, :], in_=rs2[:, :], func=AF.Ln))
                    _wait(sc, sa)
                    sa.inc(sc.activation(out=rs2[:, :], in_=rs2[:, :], func=AF.Exp,
                                         scale=-0.5))
                @block.vector
                def _(v, c=c, cc=cc):
                    _wait(v, sa)
                    sv.inc(v.tensor_scalar(out=obuf[:, :],
                                           in0=ffnq[:, cc*H:(cc+1)*H],
                                           scalar1=bnag2[:, 0:1], scalar2=rs2[:, 0:1],
                                           op0=ALU.subtract, op1=ALU.mult))
                    _wait(v, sv)
                    sv.inc(v.tensor_tensor(out=obuf[:, :], in0=obuf[:, :],
                                           in1=x2q[:, cc*H:(cc+1)*H], op=ALU.add))

                @block.sync
                def _(s, c=c):
                    _wait(s, sv)
                    sout.inc(s.dma_start(out=g["out"][c*128:(c+1)*128, :],
                                         in_=obuf[:, :]), 16)


def phase_final(nc, g):
    block = g["block"]

    @block.sync
    def _(s):
        _wait(s, g["sv"], g["sa"], g["stn"])
        s.wait_ge(g["sout"].sem, g["sout"].n)


def make_consts():
    i = np.arange(128)
    triu = (i[:, None] <= i[None, :]).astype(ml_dtypes.bfloat16)
    ltri = (i[:, None] > i[None, :]).astype(ml_dtypes.bfloat16)
    trif = (i[:, None] <= i[None, :]).astype(np.float32)
    idb = np.eye(128, dtype=ml_dtypes.bfloat16)
    idf = np.eye(128, dtype=np.float32)
    return triu, ltri, trif, idb, idf


def make_in_maps(inputs):
    x = np.ascontiguousarray(np.asarray(inputs["L0_attn_hidden_states"]))
    w_in = np.ascontiguousarray(np.asarray(inputs["L0_attn_W_in"]))
    cw = np.ascontiguousarray(
        np.asarray(inputs["L0_attn_conv_w"]).reshape(66, 128, 4).transpose(1, 0, 2).reshape(128, 264))
    dtb = np.ascontiguousarray(np.asarray(inputs["L0_attn_dt_bias"]))
    alog = np.ascontiguousarray(np.asarray(inputs["L0_attn_A_log"]))
    dpar = np.ascontiguousarray(np.asarray(inputs["L0_attn_D_param"]))
    nw = np.ascontiguousarray(np.asarray(inputs["L0_attn_norm_weight"]).reshape(NCT_X, 128))
    w_out = np.ascontiguousarray(np.asarray(inputs["L0_attn_W_out"]))
    fpk = np.ascontiguousarray(np.asarray(inputs["L0_ffn_weight_packed"]))
    fsc = np.ascontiguousarray(np.asarray(inputs["L0_ffn_scales"]))
    triu, ltri, trif, idb, idf = make_consts()

    in_maps = []
    for core in range(8):
        b, seg = divmod(core, 4)
        s0 = seg * T
        xs = np.zeros((TEXT, H), dtype=np.float32)
        xs[3:] = x[b, s0:s0 + T]
        if seg > 0:
            xs[0:3] = x[b, s0 - 3:s0]
        segmask = np.broadcast_to((np.arange(4) < seg).astype(np.float32), (128, 4)).copy()
        in_maps.append({
            "xseg": xs, "w_in": w_in, "cw": cw, "dtb": dtb, "alog": alog,
            "dpar": dpar, "nw": nw, "w_out": w_out, "fpk": fpk, "fsc": fsc,
            "triu": triu, "ltri": ltri, "trif": trif, "idb": idb, "idf": idf,
            "eyef": idf, "segm": segmask,
        })
    return in_maps


def kernel(**inputs):
    nc = build_program()
    in_maps = make_in_maps(inputs)
    res = run_bass_kernel_spmd(nc, in_maps, core_ids=list(range(8)))
    out = np.empty((2, 8192, H), dtype=np.float32)
    for core in range(8):
        b, seg = divmod(core, 4)
        out[b, seg * T:(seg + 1) * T] = np.asarray(res.results[core]["out"])
    return out
